# version 24
# baseline (speedup 1.0000x reference)
import numpy as np
import ml_dtypes
import concourse.bass as bass
import concourse.mybir as mybir
from concourse.bass_utils import run_bass_kernel_spmd

F32 = mybir.dt.float32
BF16 = mybir.dt.bfloat16
I32 = mybir.dt.int32
AF = mybir.ActivationFunctionType
ALU = mybir.AluOpType
AX = mybir.AxisListType

B, T, M, D, G, H = 4, 2048, 2048, 128, 4, 4
NT = 16
OWN = 8
MCN = 16
HEADS = 16
HID = 5632
HCN = 44
S4 = float(D) ** -0.25
EPS = 1e-5
NEG = -1.0e30
KB = 1024


class Res:
    __slots__ = ("w", "r")

    def __init__(self):
        self.w = None
        self.r = []


class Prog:
    ENGS = ("T", "V", "S", "Q", "Y")
    NLANE = 8
    DMAQ = tuple(f"DQ{i}" for i in range(8)) + tuple(f"DG{i}" for i in range(8))
    DOMS = ENGS + DMAQ

    def __init__(self):
        self.streams = {e: [] for e in self.ENGS}
        self.counts = dict.fromkeys(self.DOMS, 0)
        self.qn = {"DQ": 0, "DG": 0}

    def op(self, eng, emit, reads=(), writes=(), dma=None, deps=()):
        lane_prev = 0
        if dma:
            n = self.qn[dma]
            self.qn[dma] = n + 1
            dom = f"{dma}{n % self.NLANE}"
            lane_prev = n // self.NLANE
        else:
            dom = eng
        dd = {}

        def add(tok):
            if tok is None:
                return
            d, c = tok
            if d == eng and eng not in ("V", "S"):
                return
            if dd.get(d, 0) < c:
                dd[d] = c

        for r in reads:
            add(r.w)
        for w in writes:
            add(w.w)
            for t in w.r:
                add(t)
        for t in deps:
            add(t)
        if lane_prev > 0 and dd.get(dom, 0) < lane_prev:
            dd[dom] = lane_prev
        self.counts[dom] += 1
        tok = (dom, self.counts[dom])
        for r in reads:
            r.r.append(tok)
        for w in writes:
            w.w = tok
            w.r = []
        self.streams[eng].append((dd, emit, dom))
        return tok

    def barrier(self):
        snap = dict(self.counts)
        for e in self.ENGS:
            dd = {d: snap[d] for d in self.DOMS if d != e and snap[d] > 0}
            self.streams[e].append((dd, None, e))

    def finish(self, nc, block, sems):
        self.barrier()

        def mk(e):
            def body(eng):
                last = dict.fromkeys(self.DOMS, 0)
                for dd, fn, dom in self.streams[e]:
                    for d in sorted(dd):
                        c = dd[d]
                        if c > last[d]:
                            eng.wait_ge(sems[d], c * (16 if d[:2] in ("DQ", "DG") else 1))
                            last[d] = c
                    if fn is not None:
                        ins = fn(eng)
                        ins.then_inc(sems[dom], 16 if dom[:2] in ("DQ", "DG") else 1)
            return body

        block.tensor(mk("T"))
        block.vector(mk("V"))
        block.scalar(mk("S"))
        block.gpsimd(mk("Q"))
        block.sync(mk("Y"))


def build_nc(upto="all"):
    nc = bass.Bass("TRN2", target_bir_lowering=False)
    pr = Prog()

    dx = nc.dram_tensor("x_kv", [T, M], F32, kind="ExternalInput").ap()
    dxo = nc.dram_tensor("x_own", [OWN * 128, M], F32, kind="ExternalInput").ap()
    dwaq = nc.dram_tensor("w_aq", [G, H, M, D], F32, kind="ExternalInput").ap()
    dwak = nc.dram_tensor("w_ak", [G, M, D], F32, kind="ExternalInput").ap()
    dwav = nc.dram_tensor("w_av", [G, M, D], F32, kind="ExternalInput").ap()
    dwao = nc.dram_tensor("w_ao", [G, H, D, M], F32, kind="ExternalInput").ap()
    dwqg = nc.dram_tensor("wq_gate_t", [M, HID], I32, kind="ExternalInput").ap()
    dwqu = nc.dram_tensor("wq_up_t", [M, HID], I32, kind="ExternalInput").ap()
    dwqd = nc.dram_tensor("wq_down_t", [HID, M], I32, kind="ExternalInput").ap()
    dsg = nc.dram_tensor("s_gate_t", [128, HCN], F32, kind="ExternalInput").ap()
    dsu = nc.dram_tensor("s_up_t", [128, HCN], F32, kind="ExternalInput").ap()
    dsd = nc.dram_tensor("s_down_r", [1, M], F32, kind="ExternalInput").ap()
    dck = nc.dram_tensor("tab_ck", [128, NT * 128], BF16, kind="ExternalInput").ap()
    dsk = nc.dram_tensor("tab_sk", [128, NT * 128], BF16, kind="ExternalInput").ap()
    dcq = nc.dram_tensor("tab_cq", [128, OWN * 128], BF16, kind="ExternalInput").ap()
    dsq = nc.dram_tensor("tab_sq", [128, OWN * 128], BF16, kind="ExternalInput").ap()
    dmask = nc.dram_tensor("masks", [128, OWN * 256], F32, kind="ExternalInput").ap()
    dident = nc.dram_tensor("ident", [128, 128], BF16, kind="ExternalInput").ap()
    dones = nc.dram_tensor("ones_bf", [128, 128], BF16, kind="ExternalInput").ap()
    dout = nc.dram_tensor("out", [OWN * 128, M], F32, kind="ExternalOutput").ap()

    drbuf = nc.dram_tensor("r_buf", [OWN * 128, M], F32, kind="Internal").ap()

    ident = nc.alloc_sbuf_tensor("s_ident", [128, 128], BF16)
    ones = nc.alloc_sbuf_tensor("s_ones", [128, 128], BF16)
    stats = nc.alloc_sbuf_tensor("s_stats", [128, 64], F32)
    lnv = nc.alloc_sbuf_tensor("s_lnv", [128, 16], F32)
    kgs = nc.alloc_sbuf_tensor("s_kgs", [128, 8], F32)
    sqn = nc.alloc_sbuf_tensor("s_sqn", [128, 128], F32)
    qn2 = nc.alloc_sbuf_tensor("s_qn2", [128, 32], F32)
    nbv = nc.alloc_sbuf_tensor("s_negB", [128, 16], F32)
    zpv = nc.alloc_sbuf_tensor("s_zp", [128, 8], F32)
    rzv = nc.alloc_sbuf_tensor("s_rz", [128, 4], F32)
    prow = [nc.alloc_sbuf_tensor(f"s_prow{i}", [128, 2048], BF16) for i in range(2)]
    sgt = nc.alloc_sbuf_tensor("s_sgt", [128, 64], F32)
    sut = nc.alloc_sbuf_tensor("s_sut", [128, 64], F32)
    diagrz = nc.alloc_sbuf_tensor("s_diagrz", [128, 3 * 128], BF16)
    junk = nc.alloc_sbuf_tensor("s_junk", [128, 128], F32)
    epsv = nc.alloc_sbuf_tensor("s_epsv", [128, 4], F32)

    arena0 = (nc.sbuf_base + 31) & ~31
    top = nc.sbuf_top
    assert top - arena0 >= 194 * KB, f"arena too small: {top - arena0}"

    def at(name, off_kb, shape, dtype):
        return nc.alloc_sbuf_tensor_at(name, shape, dtype,
                                       offset=arena0 + int(off_kb * KB))

    ck = at("a_ck", 0, [128, 2048], BF16)
    sk = at("a_sk", 4, [128, 2048], BF16)
    KT = at("a_KT", 16, [128, 4 * 2048], BF16)
    VT = at("a_VT", 32, [128, NT * 512], BF16)
    wakv = at("a_wakv", 48, [128, MCN * 1024], BF16)
    xst = [at(f"a_xst{i}", 80 + 8 * i, [128, 2048], F32) for i in range(2)]
    xln = [at(f"a_xln{i}", 96 + 4 * i, [128, 2048], BF16) for i in range(2)]
    xlt = [at(f"a_xlt{i}", 104 + 4 * i, [128, 2048], BF16) for i in range(2)]
    krp = [at(f"a_krp{i}", 112 + 1 * i, [128, 512], BF16) for i in range(2)]
    ksq = xlt[0]
    cq = at("a_cq", 0, [128, 1024], BF16)
    sq = at("a_sq", 2, [128, 1024], BF16)
    masks = at("a_masks", 8, [128, 2048], F32)
    qT = at("a_qT", 48, [128, HEADS * OWN * 128], BF16)
    waq = at("a_waq", 114, [128, MCN * 2048], BF16)
    qrp = at("a_qrp", 178, [128, 2048], BF16)
    tt1 = at("a_tt1", 182, [128, 1024], F32)
    tt2 = at("a_tt2", 186, [128, 1024], F32)
    qrp2 = at("a_qrp2", 190, [128, 2048], BF16)
    qbf = at("a_qbf", 4, [128, 2048], BF16)
    krw = [at(f"a_krw{i}", 8 + i, [128, 512], BF16) for i in range(2)]
    PT = [at("a_PT0", 80, [128, NT * 1024], BF16),
          at("a_PT1", 114, [128, NT * 1024], BF16)]
    oT = at("a_oT", 146, [128, HEADS * 1024], BF16)
    prow2 = at("a_prow2", 0, [128, 2048], BF16)
    wao = at("a_wao", 0, [128, HEADS * 2048], BF16)
    x2lt = at("a_x2lt", 64, [128, MCN * 1024], BF16)
    xstb = [at(f"a_xstb{i}", 96 + 8 * i, [128, 2048], F32) for i in range(2)]
    rti = [at(f"a_rti{i}", 112 + 8 * i, [128, 2048], F32) for i in range(2)]
    x2ln = at("a_x2ln", 128, [128, 2048], BF16)
    x2ln1 = at("a_x2ln1", 132, [128, 2048], BF16)
    wgbf = [at(f"b_wgbf{i}", 0 + 8 * i, [128, 2048], I32) for i in range(2)]
    wubf = [at(f"b_wubf{i}", 16 + 4 * i, [128, 2048], BF16) for i in range(2)]
    WgT = [at(f"b_WgT{i}", 24 + 4 * i, [128, 2048], BF16) for i in range(2)]
    WuT = [at(f"b_WuT{i}", 32 + 4 * i, [128, 2048], BF16) for i in range(2)]
    gsil = at("b_gsil", 40, [128, 1024], F32)
    guT = at("b_guT", 96, [128, HCN * 1024], BF16)
    wdbf = [at("d_wdbf0", 0, [128, 2048], BF16),
            at("d_wdbf1", 44, [128, 2048], I32)]
    wdsl = [at(f"d_wdsl{i}", 8 + 4 * i, [128, 2048], BF16) for i in range(2)]
    rsl = [at(f"d_rsl{i}", 16 + 2 * i, [128, 512], F32) for i in range(8)]
    sdb = at("d_sdb", 32, [128, 2048], F32)
    outst = [at(f"d_outst{i}", 40 + 2 * i, [128, 512], F32) for i in range(2)]

    ps = nc.alloc_psum_tensor("ps", [128, 4096], F32)
    psb = ps.ap().bitcast(BF16)

    def bank(b, n=1):
        return ps.ap()[:, b * 512:(b + n) * 512]

    def bankb(b, n=1):
        return psb[:, b * 1024:(b + n) * 1024]

    R = lambda: Res()
    r_xst = [R(), R()]
    r_xln = [R(), R()]
    r_xlt = [R(), R()]
    r_krp = [R(), R()]
    r_stats = [R(), R()]
    r_lnv = [R(), R()]
    r_tabs = R()
    r_masks = R()
    r_const = R()
    r_bias = R()
    r_KT, r_VT, r_wakv, r_waq, r_qT, r_oT, r_wao = R(), R(), R(), R(), R(), R(), R()
    r_ksq = R()
    r_kgs, r_sqn = R(), R()
    r_qn2 = [R(), R()]
    r_nb = [R(), R()]
    r_zp = [R(), R()]
    r_rz = [R(), R()]
    r_prow = [R(), R(), R()]
    r_diag = [R(), R(), R()]
    r_p32 = [R(), R()]
    r_PT = [R(), R()]
    r_qrp, r_qrp2, r_tt1, r_tt2 = R(), R(), R(), R()
    r_qbf = R()
    r_krw = [R(), R()]
    r_x2lt, r_x2ln, r_x2ln1 = R(), R(), R()
    r_xstb = [R(), R()]
    r_rti = [R(), R()]
    r_rbuf = [R() for _ in range(OWN)]
    r_wdtbuf = R()
    r_wgbf = [R(), R()]
    r_wubf = [R(), R()]
    r_WgT = [R(), R()]
    r_WuT = [R(), R()]
    r_gsil, r_guT = R(), R()
    r_sdb, r_sscale = R(), R()
    r_wdbf = [R(), R()]
    r_wdsl = [R(), R()]
    r_rsl = [R() for _ in range(8)]
    r_outst = [R(), R()]
    r_junk = R()
    r_bank = [R() for _ in range(8)]

    def banks_r(b, n=1):
        return r_bank[b:b + n]

    def load(dst, src, res_w, res_r=(), q="Y"):
        dma = "DQ" if q == "Y" else "DG"
        return pr.op(q, lambda e: e.dma_start(dst, src),
                     reads=list(res_r), writes=[res_w], dma=dma)

    def store(dst, src, res_r, res_w=(), q="Q"):
        dma = "DG" if q == "Q" else "DQ"
        return pr.op(q, lambda e: e.dma_start(dst, src),
                     reads=list(res_r), writes=list(res_w), dma=dma)

    def ln_stats(xsrc, r_xsrc, rot):
        st = stats.ap()[:, rot * 32:rot * 32 + 32]
        rs = r_stats[rot]
        lv = lnv.ap()[:, rot * 8:rot * 8 + 8]
        rl = r_lnv[rot]
        sink = prow[rot]
        pr.op("S", lambda e, sink=sink: e.activation(
            sink.ap(), xsrc, AF.Square, accum_out=st[:, 0:1]),
            reads=[r_xsrc], writes=[rs, r_prow[rot]])
        pr.op("S", lambda e, sink=sink: e.activation(
            sink.ap(), xsrc, AF.Identity, accum_out=st[:, 2:3]),
            reads=[r_xsrc], writes=[rs, r_prow[rot]])
        pr.op("V", lambda e: e.tensor_scalar(st[:, 3:4], st[:, 2:3],
                                             1.0 / 2048.0, None, ALU.mult),
              reads=[rs], writes=[rs])
        pr.op("V", lambda e: e.scalar_tensor_tensor(
            st[:, 4:5], st[:, 2:3], 1.0 / 2048.0, st[:, 3:4], ALU.mult, ALU.mult),
            reads=[rs], writes=[rs])
        pr.op("V", lambda e: e.scalar_tensor_tensor(
            st[:, 5:6], st[:, 0:1], 1.0 / 2048.0, st[:, 4:5], ALU.mult, ALU.subtract),
            reads=[rs], writes=[rs])
        pr.op("S", lambda e: e.activation(lv[:, 0:1], st[:, 5:6], AF.Sqrt,
                                          bias=epsv.ap()[:, 0:1]),
              reads=[rs, r_const], writes=[rl])

    def ln_fin(xsrc, r_xsrc, xdst, r_xdst, rot):
        st = stats.ap()[:, rot * 32:rot * 32 + 32]
        rs = r_stats[rot]
        lv = lnv.ap()[:, rot * 8:rot * 8 + 8]
        rl = r_lnv[rot]
        pr.op("V", lambda e: e.reciprocal(lv[:, 1:2], lv[:, 0:1]), reads=[rl], writes=[rl])
        pr.op("V", lambda e: e.scalar_tensor_tensor(lv[:, 2:3], st[:, 3:4], -1.0,
                                                    lv[:, 1:2], ALU.mult, ALU.mult),
              reads=[rs, rl], writes=[rl])
        pr.op("S", lambda e: e.activation(xdst, xsrc, AF.Identity,
                                          bias=lv[:, 2:3], scale=lv[:, 1:2]),
              reads=[r_xsrc, rl], writes=[r_xdst])

    def ln_pass(xsrc, r_xsrc, xdst, r_xdst, rot):
        ln_stats(xsrc, r_xsrc, rot)
        ln_fin(xsrc, r_xsrc, xdst, r_xdst, rot)

    def transpose16(src_ap, r_src, dst_ten, dst_col_of, r_dst, pb0, n=16,
                    src_col_of=None, eng="V"):
        if src_col_of is None:
            src_col_of = lambda c: c * 128
        for c0 in range(0, n, 8):
            nb = min(8, n - c0)
            pbank = pb0 + (c0 // 8) % 2
            pv = bankb(pbank)
            for c in range(c0, c0 + nb):
                cc = c
                pr.op("T", lambda e, cc=cc, pv=pv: e.transpose(
                    pv[:, (cc % 8) * 128:(cc % 8) * 128 + 128],
                    src_ap[:, src_col_of(cc):src_col_of(cc) + 128], ident.ap()),
                    reads=[r_src, r_const], writes=[r_bank[pbank]])
            dcols = [dst_col_of(c) for c in range(c0, c0 + nb)]
            step = dcols[1] - dcols[0] if nb > 1 else 128
            assert all(dcols[i + 1] - dcols[i] == step for i in range(nb - 1))
            dap = bass.AP(dst_ten, dcols[0], [[dst_ten.shape[1], 128], [step, nb], [1, 128]])
            if eng == "V":
                pr.op("V", lambda e, pv=pv, dap=dap, nb=nb: e.tensor_copy(
                    dap, pv[:, 0:nb * 128]), reads=[r_bank[pbank]], writes=[r_dst])
            else:
                pr.op("S", lambda e, pv=pv, dap=dap, nb=nb: e.activation(
                    dap, pv[:, 0:nb * 128], AF.Identity),
                    reads=[r_bank[pbank]], writes=[r_dst])

    load(xst[0].ap(), dx[0:128, :], r_xst[0])
    load(xst[1].ap(), dx[128:256, :], r_xst[1])
    pr.op("V", lambda e: e.memset(epsv.ap(), EPS), writes=[r_const])
    pr.op("V", lambda e: e.memset(epsv.ap()[:, 2:4], 0.0), writes=[r_const])
    load(ident.ap(), dident, r_const)
    load(ones.ap(), dones, r_const)
    load(ck.ap(), dck, r_tabs)
    load(sk.ap(), dsk, r_tabs)

    for g in range(G):
        for part, src in ((0, dwak), (1, dwav)):
            sap = src[g].rearrange("(mc p) d -> p mc d", p=128)
            dap = bass.AP(wakv, part * 512 + g * 128, [[MCN * 1024, 128], [1024, MCN], [1, 128]])
            load(dap, sap, r_wakv, q="Q")

    kv_banks = [(0, 2), (2, 2)]

    r_xln3 = R()
    xln3 = [(xln[0], r_xln[0]), (xln[1], r_xln[1]), (qrp, r_xln3)]

    def a1_load(jt):
        rot = jt % 2
        load(xst[rot].ap(), dx[jt * 128:(jt + 1) * 128, :], r_xst[rot])

    def a1_stats(jt):
        rot = jt % 2
        ln_stats(xst[rot].ap(), r_xst[rot], rot)

    def a1_fin(jt):
        rot = jt % 2
        xd, r_xd = xln3[jt % 3]
        ln_fin(xst[rot].ap(), r_xst[rot], xd.ap(), r_xd, rot)

    def a1_kv(jt):
        rot = jt % 2
        xd, r_xd = xln3[jt % 3]
        transpose16(xd.ap(), r_xd, xlt[rot], lambda c: c * 128,
                    r_xlt[rot], 4)
        b0, nb = kv_banks[rot]
        kvp = bank(b0, nb)
        for mc in range(MCN):
            for piece in range(2):
                pr.op("T", lambda e, mc=mc, piece=piece, kvp=kvp, rot=rot: e.matmul(
                    kvp[:, piece * 512:(piece + 1) * 512],
                    xlt[rot].ap()[:, mc * 128:(mc + 1) * 128],
                    wakv.ap()[:, mc * 1024 + piece * 512: mc * 1024 + (piece + 1) * 512],
                    start=(mc == 0), stop=(mc == MCN - 1)),
                    reads=[r_xlt[rot], r_wakv], writes=banks_r(b0 + piece))
        pr.op("V", lambda e, kvp=kvp, rot=rot: e.tensor_copy(
            krw[rot].ap(), kvp[:, 0:512]),
            reads=banks_r(b0, 1), writes=[r_krw[rot]])
        kp = krw[rot].ap()
        kswap = kp.rearrange("p (g two half) -> p g two half", two=2, half=64)[:, :, ::-1, :]
        ckt = ck.ap()[:, jt * 128:(jt + 1) * 128].rearrange(
            "p (two half) -> p two half", two=2).unsqueeze(1).broadcast_to([128, G, 2, 64])
        skt = sk.ap()[:, jt * 128:(jt + 1) * 128].rearrange(
            "p (two half) -> p two half", two=2).unsqueeze(1).broadcast_to([128, G, 2, 64])
        t1v = tt1.ap().bitcast(BF16)[:, 0:512]
        t2v = tt2.ap().bitcast(BF16)[:, 0:512]
        kro = krp[rot].ap()
        pr.op("V", lambda e, kp=kp, ckt=ckt, t1v=t1v, rot=rot: e.tensor_tensor(
            t1v.rearrange("p (g two half) -> p g two half", two=2, half=64),
            kp.rearrange("p (g two half) -> p g two half", two=2, half=64), ckt, ALU.mult),
            reads=[r_krw[rot], r_tabs], writes=[r_tt1])
        pr.op("V", lambda e, kswap=kswap, skt=skt, t2v=t2v, rot=rot: e.tensor_tensor(
            t2v.rearrange("p (g two half) -> p g two half", two=2, half=64),
            kswap, skt, ALU.mult),
            reads=[r_krw[rot], r_tabs], writes=[r_tt2])
        pr.op("V", lambda e, t1v=t1v, t2v=t2v, kro=kro: e.tensor_tensor(
            kro, t1v, t2v, ALU.add), reads=[r_tt1, r_tt2], writes=[r_krp[rot]])
        pr.op("V", lambda e, kvp=kvp, jt=jt: e.tensor_copy(
            VT.ap()[:, jt * 512:(jt + 1) * 512], kvp[:, 512:1024]),
            reads=banks_r(b0 + 1, 1), writes=[r_VT])

    def a1_ktr(jt):
        rot = jt % 2
        transpose16(krp[rot].ap(), r_krp[rot], KT,
                    lambda c, jt=jt: c * 2048 + jt * 128, r_KT, 6, n=4)

    a1_stats(0)
    a1_fin(0)
    a1_stats(1)
    a1_fin(1)
    for jt in range(NT):
        if jt + 2 < NT:
            a1_load(jt + 2)
        a1_kv(jt)
        if jt >= 1:
            a1_ktr(jt - 1)
        if jt + 2 < NT:
            a1_stats(jt + 2)
            a1_fin(jt + 2)
    a1_ktr(NT - 1)
    for g in range(G):
        pr.op("S", lambda e, g=g: e.activation(ksq.ap(), KT.ap()[:, g * 2048:(g + 1) * 2048],
                                               AF.Square),
              reads=[r_KT], writes=[r_ksq, r_xlt[0]])
        for piece in range(4):
            pr.op("T", lambda e, piece=piece: e.matmul(
                bank(piece), ones.ap(), ksq.ap()[:, piece * 512:(piece + 1) * 512],
                start=True, stop=True),
                reads=[r_ksq, r_const], writes=banks_r(piece))
        pr.op("V", lambda e, g=g: e.tensor_reduce(
            kgs.ap()[:, g:g + 1], bank(0, 4), AX.X, ALU.max),
            reads=banks_r(0, 4), writes=[r_kgs])
    pr.op("S", lambda e: e.activation(kgs.ap()[:, 4:8], kgs.ap()[:, 0:4], AF.Sqrt),
          reads=[r_kgs], writes=[r_kgs])

    doutb = dout.bitcast(BF16)
    band = [0]

    def dump_bf(src_ap, r_src):
        n = src_ap.shape[1]
        for c0 in range(0, n, 4096):
            w = min(4096, n - c0)
            b = band[0]
            band[0] += 1
            store(doutb[b * 128:(b + 1) * 128, 0:w], src_ap[:, c0:c0 + w], [r_src])

    def dump_f32(src_ap, r_src):
        n = src_ap.shape[1]
        for c0 in range(0, n, 2048):
            w = min(2048, n - c0)
            b = band[0]
            band[0] += 1
            store(dout[b * 128:(b + 1) * 128, 0:w], src_ap[:, c0:c0 + w], [r_src])

    if upto == "a1":
        dump_bf(KT.ap(), r_KT)
        dump_bf(VT.ap(), r_VT)
        dump_f32(kgs.ap(), r_kgs)
        dump_bf(xln[1].ap(), r_xln[1])
        dump_f32(stats.ap(), r_stats[1])
        dump_f32(lnv.ap(), r_lnv[1])
        return nc, pr

    for g in range(G):
        for h in range(H):
            sap = dwaq[g, h].rearrange("(mc p) d -> p mc d", p=128)
            dap = bass.AP(waq, (g * H + h) * 128, [[MCN * 2048, 128], [2048, MCN], [1, 128]])
            load(dap, sap, r_waq, q="Q")

    qrpb = [(qrp, r_qrp), (qrp2, r_qrp2)]

    def a2q_load(k):
        rot = k % 2
        load(xst[rot].ap(), dxo[k * 128:(k + 1) * 128, :], r_xst[rot])

    def a2q_stats(k):
        rot = k % 2
        ln_stats(xst[rot].ap(), r_xst[rot], rot)

    def a2q_fin(k):
        rot = k % 2
        ln_fin(xst[rot].ap(), r_xst[rot], xln[rot].ap(), r_xln[rot], rot)

    def a2q_xlt(k):
        rot = k % 2
        transpose16(xln[rot].ap(), r_xln[rot], xlt[rot], lambda c: c * 128,
                    r_xlt[rot], 4)

    def a2q_mm(k):
        rot = k % 2
        for piece in range(4):
            for mc in range(MCN):
                pr.op("T", lambda e, mc=mc, piece=piece, rot=rot: e.matmul(
                    bank(piece),
                    xlt[rot].ap()[:, mc * 128:(mc + 1) * 128],
                    waq.ap()[:, mc * 2048 + piece * 512: mc * 2048 + (piece + 1) * 512],
                    start=(mc == 0), stop=(mc == MCN - 1)),
                    reads=[r_xlt[rot], r_waq], writes=banks_r(piece))

    def a2q_rope(k):
        rot = k % 2
        qo, r_qo = qrpb[rot]
        cqt = cq.ap()[:, k * 128:(k + 1) * 128].rearrange(
            "p (two half) -> p two half", two=2).unsqueeze(1).broadcast_to([128, HEADS, 2, 64])
        sqt = sq.ap()[:, k * 128:(k + 1) * 128].rearrange(
            "p (two half) -> p two half", two=2).unsqueeze(1).broadcast_to([128, HEADS, 2, 64])
        pr.op("V", lambda e: e.tensor_copy(qbf.ap(), bank(0, 4)),
              reads=banks_r(0, 4), writes=[r_qbf, r_tabs])
        qp = qbf.ap()
        qswap = qp.rearrange("p (gh two half) -> p gh two half",
                             two=2, half=64)[:, :, ::-1, :]
        t1v = tt1.ap().bitcast(BF16)
        t2v = tt2.ap().bitcast(BF16)
        pr.op("V", lambda e, qp=qp, cqt=cqt, t1v=t1v: e.tensor_tensor(
            t1v.rearrange("p (gh two half) -> p gh two half", two=2, half=64),
            qp.rearrange("p (gh two half) -> p gh two half", two=2, half=64),
            cqt, ALU.mult),
            reads=[r_qbf, r_tabs], writes=[r_tt1])
        pr.op("V", lambda e, qswap=qswap, sqt=sqt, t2v=t2v: e.tensor_tensor(
            t2v.rearrange("p (gh two half) -> p gh two half", two=2, half=64),
            qswap, sqt, ALU.mult),
            reads=[r_qbf, r_tabs], writes=[r_tt2])
        pr.op("V", lambda e, qo=qo, t1v=t1v, t2v=t2v: e.tensor_tensor(
            qo.ap(), t1v, t2v, ALU.add),
            reads=[r_tt1, r_tt2], writes=[r_qo])
        pr.op("S", lambda e, qo=qo, t1v=t1v: e.activation(
            t1v, qo.ap(), AF.Square), reads=[r_qo], writes=[r_tt1])
        pr.op("V", lambda e, rot=rot, t1v=t1v: e.tensor_reduce(
            qn2.ap()[:, rot * 16:(rot + 1) * 16],
            t1v.rearrange("p (h d) -> p h d", d=128), AX.X, ALU.add),
            reads=[r_tt1], writes=[r_qn2[rot]])
        pr.op("S", lambda e, k=k, rot=rot: e.activation(
            sqn.ap()[:, k * 16:(k + 1) * 16], qn2.ap()[:, rot * 16:(rot + 1) * 16],
            AF.Sqrt), reads=[r_qn2[rot]], writes=[r_sqn])

    def a2q_qtr(k):
        rot = k % 2
        qo, r_qo = qrpb[rot]
        transpose16(qo.ap(), r_qo, qT, lambda c, k=k: c * 1024 + k * 128,
                    r_qT, 6, n=HEADS)

    a2q_load(0)
    load(cq.ap(), dcq, r_tabs)
    load(sq.ap(), dsq, r_tabs)
    pr.op("Y", lambda e: e.dma_start(masks.ap(), dmask),
          writes=[r_masks, r_tabs, r_krw[0], r_krw[1]], dma="DQ")
    a2q_load(1)
    a2q_stats(0)
    a2q_fin(0)
    a2q_stats(1)
    a2q_fin(1)
    a2q_xlt(0)
    for k in range(OWN):
        if k + 2 < OWN:
            a2q_load(k + 2)
        a2q_mm(k)
        a2q_rope(k)
        if k + 1 < OWN:
            a2q_xlt(k + 1)
        if k >= 1:
            a2q_qtr(k - 1)
        if k + 2 < OWN:
            a2q_stats(k + 2)
            a2q_fin(k + 2)
    a2q_qtr(OWN - 1)

    if upto == "a2q":
        dump_bf(qT.ap(), r_qT)
        dump_f32(sqn.ap(), r_sqn)
        return nc, pr

    slot_seq = [0]

    def next_slot():
        s = slot_seq[0]
        slot_seq[0] = (s + 1) % 3
        return s

    prow3 = [prow[0], prow[1], prow2]

    def att_negb(h):
        g = h // 4
        hr = h % 2
        nb = nbv.ap()[:, hr * 8:hr * 8 + 8]
        for k in range(OWN):
            pr.op("V", lambda e, k=k, h=h, g=g, nb=nb: e.scalar_tensor_tensor(
                nb[:, k:k + 1], sqn.ap()[:, k * 16 + h:k * 16 + h + 1], -1.0,
                kgs.ap()[:, 4 + g:5 + g], ALU.mult, ALU.mult),
                reads=[r_sqn, r_kgs], writes=[r_nb[hr]])

    def att_scores(h, k, sl, widths):
        g = h // 4
        for si, s in enumerate(sl):
            wdt = widths[si]
            for c0 in range(0, wdt, 512):
                cw = min(512, wdt - c0)
                jcol = si * 1024 + c0
                pr.op("T", lambda e, s=s, c0=c0, cw=cw, jcol=jcol, h=h, k=k, g=g: e.matmul(
                    bank(2 * s, 2)[:, c0:c0 + cw],
                    qT.ap()[:, h * 1024 + k * 128: h * 1024 + (k + 1) * 128],
                    KT.ap()[:, g * 2048 + jcol: g * 2048 + jcol + cw],
                    start=True, stop=True),
                    reads=[r_qT, r_KT], writes=banks_r(2 * s + c0 // 512))

    def att_chain(h, k, sl, widths):
        hr = h % 2
        par = h % 2
        nb = nbv.ap()[:, hr * 8:hr * 8 + 8]
        J = 2 * k + 2
        W = J * 128
        zrot = k % 2
        zp = zpv.ap()[:, zrot * 4: zrot * 4 + 4]
        r_z = r_zp[zrot]
        rz = rzv.ap()[:, zrot * 2:zrot * 2 + 2]
        r_r = r_rz[zrot]
        mi = (W - 256) // 1024
        moff = (W - 256) % 1024
        pr.op("V", lambda e, s=sl[mi], moff=moff, k=k: e.tensor_tensor(
            bank(2 * s, 2)[:, moff:moff + 256],
            bank(2 * s, 2)[:, moff:moff + 256],
            masks.ap()[:, k * 256:(k + 1) * 256], ALU.add),
            reads=[r_masks] + banks_r(2 * sl[mi] + moff // 512),
            writes=banks_r(2 * sl[mi] + moff // 512))
        pw = prow3[k % 3]
        r_pw = r_prow[k % 3]
        for si, s in enumerate(sl):
            wdt = widths[si]
            pr.op("S", lambda e, s=s, si=si, wdt=wdt, zp=zp, nb=nb, k=k, pw=pw: e.activation(
                pw.ap()[:, si * 1024: si * 1024 + wdt], bank(2 * s, 2)[:, 0:wdt],
                AF.Exp, bias=nb[:, k:k + 1], accum_out=zp[:, si:si + 1]),
                reads=banks_r(2 * s, (wdt + 511) // 512) + [r_nb[hr]],
                writes=[r_pw, r_z])
        if len(sl) == 2:
            pr.op("V", lambda e, zp=zp: e.tensor_tensor(
                zp[:, 2:3], zp[:, 0:1], zp[:, 1:2], ALU.add),
                reads=[r_z], writes=[r_z])
            zi = 2
        else:
            zi = 0
        pr.op("V", lambda e, zp=zp, rz=rz, zi=zi: e.reciprocal(rz[:, 0:1], zp[:, zi:zi + 1]),
              reads=[r_z], writes=[r_r])
        pr.op("V", lambda e, k=k, rz=rz: e.tensor_scalar(
            diagrz.ap()[:, (k % 3) * 128:(k % 3) * 128 + 128], ident.ap(),
            rz[:, 0:1], None, ALU.mult),
            reads=[r_r, r_const], writes=[r_diag[k % 3]])

    def att_pt(h, k, sl):
        par = h % 2
        pt_t = PT[par]
        r_pt = r_PT[par]
        pw = prow3[k % 3]
        dg = diagrz.ap()[:, (k % 3) * 128:(k % 3) * 128 + 128]
        J = 2 * k + 2
        for j in range(J):
            slot = sl[0] if j < 8 else sl[1]
            pb = 2 * slot + (j % 8) // 4
            pv = bank(pb)
            pr.op("T", lambda e, j=j, pv=pv, pw=pw, dg=dg: e.matmul(
                pv[:, (j % 4) * 128:(j % 4) * 128 + 128],
                pw.ap()[:, j * 128:(j + 1) * 128], dg, start=True, stop=True),
                reads=[r_prow[k % 3], r_diag[k % 3]], writes=[r_bank[pb]])
        for j0 in range(0, J, 8):
            nj = min(8, J - j0)
            slot = sl[0] if j0 < 8 else sl[1]
            dap = bass.AP(pt_t, j0 * 1024 + k * 128,
                          [[NT * 1024, 128], [1024, nj], [1, 128]])
            eng = "V" if (j0 // 8 + k) % 2 == 0 else "S"
            if eng == "V":
                pr.op("V", lambda e, slot=slot, nj=nj, dap=dap: e.tensor_copy(
                    dap, bank(2 * slot, 2)[:, 0:nj * 128]),
                    reads=banks_r(2 * slot, 2), writes=[r_pt])
            else:
                pr.op("S", lambda e, slot=slot, nj=nj, dap=dap: e.activation(
                    dap, bank(2 * slot, 2)[:, 0:nj * 128], AF.Identity),
                    reads=banks_r(2 * slot, 2), writes=[r_pt])

    def att_pv_ops(h):
        g = h // 4
        par = h % 2
        pt_t = PT[par]
        r_pt = r_PT[par]
        thunks = []
        for jt in range(NT):
            k0 = jt // 2
            c0 = k0 * 128
            pieces = []
            if c0 < 512:
                pieces.append((c0, 512))
            pieces.append((max(c0, 512), 1024))
            for (a, b) in pieces:
                pbank = 6 + (a // 512)

                def th(jt=jt, a=a, b=b, pbank=pbank):
                    pr.op("T", lambda e, jt=jt, a=a, b=b: e.matmul(
                        bank(6, 2)[:, a:b],
                        VT.ap()[:, jt * 512 + g * 128: jt * 512 + (g + 1) * 128],
                        pt_t.ap()[:, jt * 1024 + a: jt * 1024 + b],
                        start=(jt == 0), stop=(jt == (7 if pbank == 6 else 15))),
                        reads=[r_VT, r_pt], writes=[r_bank[pbank]])
                thunks.append(th)
        return thunks

    def att_oevac(h):
        pr.op("S", lambda e, h=h: e.activation(
            oT.ap()[:, h * 1024:(h + 1) * 1024], bank(6, 2), AF.Identity),
            reads=banks_r(6, 2), writes=[r_oT])

    pending_pv = []
    pending_h = None
    for h in range(HEADS):
        att_negb(h)
        slk = {}
        wdk = {}
        for k in range(OWN):
            W = (2 * k + 2) * 128
            nslots = 1 if W <= 1024 else 2
            sl = [next_slot() for _ in range(nslots)]
            slk[k] = sl
            wdk[k] = [min(W, 1024)] + ([W - 1024] if W > 1024 else [])
            att_scores(h, k, sl, wdk[k])
            for _ in range(3):
                if pending_pv:
                    pending_pv.pop(0)()
            att_chain(h, k, sl, wdk[k])
            if k >= 2:
                att_pt(h, k - 2, slk[k - 2])
        att_pt(h, OWN - 2, slk[OWN - 2])
        att_pt(h, OWN - 1, slk[OWN - 1])
        while pending_pv:
            pending_pv.pop(0)()
        if pending_h is not None:
            att_oevac(pending_h)
        pending_pv = att_pv_ops(h)
        pending_h = h
    while pending_pv:
        pending_pv.pop(0)()
    att_oevac(pending_h)

    if upto == "a2att":
        dump_bf(oT.ap(), r_oT)
        return nc, pr

    pr.barrier()
    st_wao = at("a_stwao", 136, [128, 2048], F32)
    r_stw = R()
    r_wao_hh = [R() for _ in range(HEADS)]
    for h in range(HEADS):
        g, hh = h // 4, h % 4
        dst = bass.AP(wao, h * 2048, [[HEADS * 2048, 128], [1, 2048]])
        if h % 2 == 1:
            load(dst, dwao[g, hh], r_wao_hh[h], q="Q")
        else:
            load(st_wao.ap(), dwao[g, hh], r_stw, q="Y")
            pr.op("S", lambda e, dst=dst: e.copy(dst, st_wao.ap()),
                  reads=[r_stw], writes=[r_wao_hh[h], r_stw])
    def a2b_mm(k):
        seto = (k % 2) * 4
        for h in range(HEADS):
            for piece in range(4):
                pr.op("T", lambda e, h=h, piece=piece, k=k, seto=seto: e.matmul(
                    bank(seto + piece),
                    oT.ap()[:, h * 1024 + k * 128: h * 1024 + (k + 1) * 128],
                    wao.ap()[:, h * 2048 + piece * 512: h * 2048 + (piece + 1) * 512],
                    start=(h == 0), stop=(h == HEADS - 1)),
                    reads=[r_oT, r_wao_hh[h]], writes=banks_r(seto + piece))

    x2lnbuf = [(x2ln, r_x2ln), (x2ln1, r_x2ln1)]
    load(xstb[0].ap(), dxo[0:128, :], r_xstb[0])
    for k in range(OWN):
        rot = k % 2
        seto = rot * 4
        if k + 1 < OWN:
            load(xstb[1 - rot].ap(), dxo[(k + 1) * 128:(k + 2) * 128, :],
                 r_xstb[1 - rot], res_r=[r_xstb[1 - rot]])
        a2b_mm(k)
        if k >= 1:
            ln_fin(rti[1 - rot].ap(), r_rti[1 - rot],
                   x2lnbuf[(k - 1) % 2][0].ap(), x2lnbuf[(k - 1) % 2][1],
                   (k - 1) % 2)
            transpose16(x2lnbuf[(k - 1) % 2][0].ap(), x2lnbuf[(k - 1) % 2][1],
                        x2lt, lambda c, kk=k - 1: c * 1024 + kk * 128,
                        r_x2lt, ((k - 1) % 2) * 4)
        pr.op("V", lambda e, rot=rot, seto=seto: e.tensor_tensor(
            rti[rot].ap(), bank(seto, 4), xstb[rot].ap(), ALU.add),
            reads=banks_r(seto, 4) + [r_xstb[rot]], writes=[r_rti[rot]])
        ln_stats(rti[rot].ap(), r_rti[rot], rot)
        store(drbuf[k * 128:(k + 1) * 128, :], rti[rot].ap(), [r_rti[rot]],
              res_w=[r_rbuf[k]])
    ln_fin(rti[(OWN - 1) % 2].ap(), r_rti[(OWN - 1) % 2],
           x2lnbuf[(OWN - 1) % 2][0].ap(), x2lnbuf[(OWN - 1) % 2][1],
           (OWN - 1) % 2)
    transpose16(x2lnbuf[(OWN - 1) % 2][0].ap(), x2lnbuf[(OWN - 1) % 2][1],
                x2lt, lambda c, kk=OWN - 1: c * 1024 + kk * 128,
                r_x2lt, ((OWN - 1) % 2) * 4)

    if upto == "a2b":
        dump_bf(x2lt.ap(), r_x2lt)
        return nc, pr

    pr.barrier()
    load(sgt.ap()[:, 0:HCN], dsg, r_sscale)
    load(sut.ap()[:, 0:HCN], dsu, r_sscale)

    def b1_loads(hc):
        rot = hc % 2
        for (dsrc, dst, r_dst, q) in ((dwqg, wgbf, r_wgbf, "Y"),
                                      (dwqu, wubf, r_wubf, "Q")):
            src = dsrc[:, hc * 128:(hc + 1) * 128].rearrange(
                "(mc p) h -> p mc h", p=128)
            load(dst[rot].ap().rearrange("p (mc h) -> p mc h", h=128), src,
                 r_dst[rot], q=q)

    def b1_deq(hc):
        rot = hc % 2
        for (stg, r_stg, WT, rWT) in ((wgbf, r_wgbf, WgT, r_WgT),
                                      (wubf, r_wubf, WuT, r_WuT)):
            pr.op("V", lambda e, rot=rot, stg=stg, WT=WT: e.tensor_scalar(
                WT[rot].ap(), stg[rot].ap(), -128.0, None, ALU.add),
                reads=[r_stg[rot]], writes=[rWT[rot]])

    def b1_mm(hc):
        rot = hc % 2
        seta = 0 if hc % 2 == 0 else 4
        for WT, rWT, boff in ((WgT, r_WgT, 0), (WuT, r_WuT, 2)):
            for mc in range(MCN):
                for piece in range(2):
                    pr.op("T", lambda e, mc=mc, piece=piece, rot=rot, seta=seta,
                          WT=WT, boff=boff: e.matmul(
                        bank(seta + boff + piece),
                        WT[rot].ap()[:, mc * 128:(mc + 1) * 128],
                        x2lt.ap()[:, mc * 1024 + piece * 512: mc * 1024 + (piece + 1) * 512],
                        start=(mc == 0), stop=(mc == MCN - 1)),
                        reads=[rWT[rot], r_x2lt], writes=banks_r(seta + boff + piece))

    def b1_post(hc):
        seta = 0 if hc % 2 == 0 else 4
        pr.op("S", lambda e, seta=seta, hc=hc: e.activation(
            gsil.ap(), bank(seta, 2), AF.Silu, scale=sgt.ap()[:, hc:hc + 1]),
            reads=banks_r(seta, 2) + [r_sscale], writes=[r_gsil])
        pr.op("V", lambda e, seta=seta, hc=hc: e.scalar_tensor_tensor(
            guT.ap()[:, hc * 1024:(hc + 1) * 1024], bank(seta + 2, 2),
            sut.ap()[:, hc:hc + 1], gsil.ap(), ALU.mult, ALU.mult),
            reads=banks_r(seta + 2, 2) + [r_gsil, r_sscale], writes=[r_guT])

    b1_loads(0)
    b1_deq(0)
    for hc in range(HCN):
        if hc + 1 < HCN:
            b1_loads(hc + 1)
        b1_mm(hc)
        b1_post(hc)
        if hc + 1 < HCN:
            b1_deq(hc + 1)

    if upto == "b1":
        dump_bf(guT.ap()[:, 0:32768], r_guT)
        return nc, pr

    pr.barrier()
    NGRP = 11
    sdsrc = bass.AP(dsd.tensor, 0, [[0, 128], [1, M]])
    load(sdb.ap(), sdsrc, r_sdb)

    def down_wload(mc4, hcg):
        rot = (mc4 * NGRP + hcg) % 2
        src = dwqd[4 * hcg * 128:(4 * hcg + 4) * 128,
                   mc4 * 512:(mc4 + 1) * 512].rearrange(
            "(hh p) m -> p hh m", p=128)
        load(wdbf[rot].ap().rearrange("p (hh m) -> p hh m", m=512), src,
             r_wdbf[rot], q="Q" if rot == 0 else "Y")

    def down_deq(mc4, hcg):
        rot = (mc4 * NGRP + hcg) % 2
        pr.op("V", lambda e, rot=rot: e.tensor_scalar(
            wdsl[rot].ap(), wdbf[rot].ap(), -128.0, None, ALU.add),
            reads=[r_wdbf[rot]], writes=[r_wdsl[rot]])

    def down_mm(mc4, hcg):
        rot = (mc4 * NGRP + hcg) % 2
        for hh in range(4):
            hc = 4 * hcg + hh
            for t in range(OWN):
                pr.op("T", lambda e, hc=hc, hh=hh, t=t, rot=rot: e.matmul(
                    bank(t),
                    guT.ap()[:, hc * 1024 + t * 128: hc * 1024 + (t + 1) * 128],
                    wdsl[rot].ap()[:, hh * 512:(hh + 1) * 512],
                    start=(hc == 0), stop=(hc == HCN - 1)),
                    reads=[r_guT, r_wdsl[rot]], writes=[r_bank[t]])

    for t in range(OWN):
        load(rsl[t].ap(), drbuf[t * 128:(t + 1) * 128, 0:512], r_rsl[t])
    down_wload(0, 0)
    down_deq(0, 0)
    for mc4 in range(4):
        for hcg in range(NGRP):
            nxt = (mc4, hcg + 1) if hcg + 1 < NGRP else (
                (mc4 + 1, 0) if mc4 + 1 < 4 else None)
            if nxt:
                down_wload(*nxt)
            down_mm(mc4, hcg)
            if nxt:
                down_deq(*nxt)
        for t in range(OWN):
            orot = t % 2
            pr.op("V", lambda e, t=t, orot=orot, mc4=mc4: e.tensor_tensor(
                outst[orot].ap(), bank(t),
                sdb.ap()[:, mc4 * 512:(mc4 + 1) * 512], ALU.mult),
                reads=[r_bank[t], r_sdb], writes=[r_outst[orot]])
            pr.op("V", lambda e, t=t, orot=orot: e.tensor_tensor(
                outst[orot].ap(), outst[orot].ap(), rsl[t].ap(), ALU.add),
                reads=[r_rsl[t], r_outst[orot]], writes=[r_outst[orot]])
            store(dout[t * 128:(t + 1) * 128, mc4 * 512:(mc4 + 1) * 512],
                  outst[orot].ap(), [r_outst[orot]])
            if mc4 + 1 < 4:
                load(rsl[t].ap(),
                     drbuf[t * 128:(t + 1) * 128, (mc4 + 1) * 512:(mc4 + 2) * 512],
                     r_rsl[t], res_r=[r_rsl[t]])

    return nc, pr




def build(upto="all"):
    nc, pr = build_nc(upto)
    sems = {}
    from contextlib import ExitStack
    ctx = ExitStack()
    for d in Prog.DOMS:
        sems[d] = ctx.enter_context(nc.semaphore(f"sem_{d}"))
    block = ctx.enter_context(nc.Block())
    try:
        pr.finish(nc, block, sems)
    finally:
        ctx.close()
    return nc


_cache = {}


def _host_tables():
    if "tabs" in _cache:
        return _cache["tabs"]
    half = D // 2
    pos = np.arange(T, dtype=np.float64)[:, None]
    freqs = (10000.0 ** (-np.arange(half, dtype=np.float64) / half))[None, :]
    ang = pos * freqs
    cosa = (np.cos(ang) * S4).astype(np.float32)
    sina = (np.sin(ang) * S4).astype(np.float32)

    def mk_tabs(tiles):
        n = len(tiles)
        C = np.empty((128, n * 128), np.float32)
        Sg = np.empty((128, n * 128), np.float32)
        for idx, t in enumerate(tiles):
            rows = slice(t * 128, (t + 1) * 128)
            C[:, idx * 128: idx * 128 + 64] = cosa[rows]
            C[:, idx * 128 + 64: idx * 128 + 128] = cosa[rows]
            Sg[:, idx * 128: idx * 128 + 64] = -sina[rows]
            Sg[:, idx * 128 + 64: idx * 128 + 128] = sina[rows]
        return C, Sg

    ck_t, sk_t = mk_tabs(list(range(NT)))
    cq0, sq0 = mk_tabs([2 * k for k in range(OWN)])
    cq1, sq1 = mk_tabs([2 * k + 1 for k in range(OWN)])

    ii = np.arange(128)[:, None]
    jj = np.arange(128)[None, :]
    tri = np.where(ii < jj, np.float32(NEG), np.float32(0.0))
    full = np.full((128, 128), np.float32(NEG))
    zero = np.zeros((128, 128), np.float32)
    m0 = np.empty((128, OWN * 256), np.float32)
    m1 = np.empty((128, OWN * 256), np.float32)
    for k in range(OWN):
        m0[:, k * 256:k * 256 + 128] = tri
        m0[:, k * 256 + 128:k * 256 + 256] = full
        m1[:, k * 256:k * 256 + 128] = zero
        m1[:, k * 256 + 128:k * 256 + 256] = tri
    cast = lambda a: a.astype(ml_dtypes.bfloat16)
    _cache["tabs"] = (cast(ck_t), cast(sk_t), (cast(cq0), cast(sq0)),
                      (cast(cq1), cast(sq1)), (m0, m1))
    return _cache["tabs"]


def kernel(L0_attn_x, L0_attn_w_aq, L0_attn_w_ak, L0_attn_w_av, L0_attn_w_ao,
           L0_ffn_wq_gate, L0_ffn_s_gate, L0_ffn_wq_up, L0_ffn_s_up,
           L0_ffn_wq_down, L0_ffn_s_down):
    x = np.ascontiguousarray(np.asarray(L0_attn_x, dtype=np.float32))
    waq = np.ascontiguousarray(np.asarray(L0_attn_w_aq, dtype=np.float32))
    wak = np.ascontiguousarray(np.asarray(L0_attn_w_ak, dtype=np.float32))
    wav = np.ascontiguousarray(np.asarray(L0_attn_w_av, dtype=np.float32))
    wao = np.ascontiguousarray(np.asarray(L0_attn_w_ao, dtype=np.float32))
    wqg = np.ascontiguousarray(np.asarray(L0_ffn_wq_gate, dtype=np.int32))
    wqu = np.ascontiguousarray(np.asarray(L0_ffn_wq_up, dtype=np.int32))
    wqd = np.ascontiguousarray(np.asarray(L0_ffn_wq_down, dtype=np.int32))
    sg = np.asarray(L0_ffn_s_gate, dtype=np.float32)
    su = np.asarray(L0_ffn_s_up, dtype=np.float32)
    sd = np.asarray(L0_ffn_s_down, dtype=np.float32)

    wqg_t = np.ascontiguousarray(wqg.T)
    wqu_t = np.ascontiguousarray(wqu.T)
    wqd_t = np.ascontiguousarray(wqd.T)
    sg_t = np.ascontiguousarray(sg.reshape(HCN, 128).T)
    su_t = np.ascontiguousarray(su.reshape(HCN, 128).T)
    sd_r = np.ascontiguousarray(sd.reshape(1, M))
    ck_t, sk_t, cqsq0, cqsq1, (m0, m1) = _host_tables()
    ident = np.eye(128, dtype=ml_dtypes.bfloat16)
    ones = np.ones((128, 128), dtype=ml_dtypes.bfloat16)

    if "nc" not in _cache:
        _cache["nc"] = build("all")
    nc = _cache["nc"]

    in_maps = []
    row_maps = []
    for c in range(8):
        b, p = c // 2, c % 2
        tiles = [2 * k + p for k in range(OWN)]
        rows = np.concatenate([np.arange(t * 128, (t + 1) * 128) for t in tiles])
        row_maps.append((b, rows))
        cqt, sqt = cqsq0 if p == 0 else cqsq1
        in_maps.append({
            "x_kv": x[b],
            "x_own": np.ascontiguousarray(x[b][rows]),
            "w_aq": waq, "w_ak": wak, "w_av": wav, "w_ao": wao,
            "wq_gate_t": wqg_t, "wq_up_t": wqu_t, "wq_down_t": wqd_t,
            "s_gate_t": sg_t, "s_up_t": su_t, "s_down_r": sd_r,
            "tab_ck": ck_t, "tab_sk": sk_t, "tab_cq": cqt, "tab_sq": sqt,
            "masks": m0 if p == 0 else m1,
            "ident": ident, "ones_bf": ones,
        })

    res = run_bass_kernel_spmd(nc, in_maps, core_ids=list(range(8)))
    out = np.empty((B, T, M), np.float32)
    for c in range(8):
        b, rows = row_maps[c]
        out[b][rows] = res.results[c]["out"]
    return out


# version 25 (final)
# speedup vs baseline: 1.0135x; 1.0135x over previous
import numpy as np
import ml_dtypes
import concourse.bass as bass
import concourse.mybir as mybir
from concourse.bass_utils import run_bass_kernel_spmd

F32 = mybir.dt.float32
BF16 = mybir.dt.bfloat16
I32 = mybir.dt.int32
AF = mybir.ActivationFunctionType
ALU = mybir.AluOpType
AX = mybir.AxisListType

B, T, M, D, G, H = 4, 2048, 2048, 128, 4, 4
NT = 16
OWN = 8
MCN = 16
HEADS = 16
HID = 5632
HCN = 44
S4 = float(D) ** -0.25
EPS = 1e-5
NEG = -1.0e30
KB = 1024


class Res:
    __slots__ = ("w", "r")

    def __init__(self):
        self.w = None
        self.r = []


class Prog:
    ENGS = ("T", "V", "S", "Q", "Y")
    NLANE = 8
    DMAQ = tuple(f"DQ{i}" for i in range(8)) + tuple(f"DG{i}" for i in range(8))
    DOMS = ENGS + DMAQ

    def __init__(self):
        self.streams = {e: [] for e in self.ENGS}
        self.counts = dict.fromkeys(self.DOMS, 0)
        self.qn = {"DQ": 0, "DG": 0}

    def op(self, eng, emit, reads=(), writes=(), dma=None, deps=()):
        lane_prev = 0
        if dma:
            n = self.qn[dma]
            self.qn[dma] = n + 1
            dom = f"{dma}{n % self.NLANE}"
            lane_prev = n // self.NLANE
        else:
            dom = eng
        dd = {}

        def add(tok):
            if tok is None:
                return
            d, c = tok
            if d == eng and eng not in ("V", "S"):
                return
            if dd.get(d, 0) < c:
                dd[d] = c

        for r in reads:
            add(r.w)
        for w in writes:
            add(w.w)
            for t in w.r:
                add(t)
        for t in deps:
            add(t)
        if lane_prev > 0 and dd.get(dom, 0) < lane_prev:
            dd[dom] = lane_prev
        self.counts[dom] += 1
        tok = (dom, self.counts[dom])
        for r in reads:
            r.r.append(tok)
        for w in writes:
            w.w = tok
            w.r = []
        self.streams[eng].append((dd, emit, dom))
        return tok

    def barrier(self):
        snap = dict(self.counts)
        for e in self.ENGS:
            dd = {d: snap[d] for d in self.DOMS if d != e and snap[d] > 0}
            self.streams[e].append((dd, None, e))

    def finish(self, nc, block, sems):
        self.barrier()

        def mk(e):
            def body(eng):
                last = dict.fromkeys(self.DOMS, 0)
                for dd, fn, dom in self.streams[e]:
                    for d in sorted(dd):
                        c = dd[d]
                        if c > last[d]:
                            eng.wait_ge(sems[d], c * (16 if d[:2] in ("DQ", "DG") else 1))
                            last[d] = c
                    if fn is not None:
                        ins = fn(eng)
                        ins.then_inc(sems[dom], 16 if dom[:2] in ("DQ", "DG") else 1)
            return body

        block.tensor(mk("T"))
        block.vector(mk("V"))
        block.scalar(mk("S"))
        block.gpsimd(mk("Q"))
        block.sync(mk("Y"))


def build_nc(upto="all"):
    nc = bass.Bass("TRN2", target_bir_lowering=False)
    pr = Prog()

    dx = nc.dram_tensor("x_kv", [T, M], F32, kind="ExternalInput").ap()
    dxo = nc.dram_tensor("x_own", [OWN * 128, M], F32, kind="ExternalInput").ap()
    dwaq = nc.dram_tensor("w_aq", [G, H, M, D], F32, kind="ExternalInput").ap()
    dwak = nc.dram_tensor("w_ak", [G, M, D], F32, kind="ExternalInput").ap()
    dwav = nc.dram_tensor("w_av", [G, M, D], F32, kind="ExternalInput").ap()
    dwao = nc.dram_tensor("w_ao", [G, H, D, M], F32, kind="ExternalInput").ap()
    dwqg = nc.dram_tensor("wq_gate_t", [M, HID], I32, kind="ExternalInput").ap()
    dwqu = nc.dram_tensor("wq_up_t", [M, HID], I32, kind="ExternalInput").ap()
    dwqd = nc.dram_tensor("wq_down_t", [HID, M], I32, kind="ExternalInput").ap()
    dsg = nc.dram_tensor("s_gate_t", [128, HCN], F32, kind="ExternalInput").ap()
    dsu = nc.dram_tensor("s_up_t", [128, HCN], F32, kind="ExternalInput").ap()
    dsd = nc.dram_tensor("s_down_r", [1, M], F32, kind="ExternalInput").ap()
    dck = nc.dram_tensor("tab_ck", [128, NT * 128], BF16, kind="ExternalInput").ap()
    dsk = nc.dram_tensor("tab_sk", [128, NT * 128], BF16, kind="ExternalInput").ap()
    dcq = nc.dram_tensor("tab_cq", [128, OWN * 128], BF16, kind="ExternalInput").ap()
    dsq = nc.dram_tensor("tab_sq", [128, OWN * 128], BF16, kind="ExternalInput").ap()
    dmask = nc.dram_tensor("masks", [128, OWN * 256], F32, kind="ExternalInput").ap()
    dident = nc.dram_tensor("ident", [128, 128], BF16, kind="ExternalInput").ap()
    dones = nc.dram_tensor("ones_bf", [128, 128], BF16, kind="ExternalInput").ap()
    dout = nc.dram_tensor("out", [OWN * 128, M], F32, kind="ExternalOutput").ap()

    drbuf = nc.dram_tensor("r_buf", [OWN * 128, M], F32, kind="Internal").ap()

    ident = nc.alloc_sbuf_tensor("s_ident", [128, 128], BF16)
    ones = nc.alloc_sbuf_tensor("s_ones", [128, 128], BF16)
    stats = nc.alloc_sbuf_tensor("s_stats", [128, 64], F32)
    lnv = nc.alloc_sbuf_tensor("s_lnv", [128, 16], F32)
    kgs = nc.alloc_sbuf_tensor("s_kgs", [128, 8], F32)
    sqn = nc.alloc_sbuf_tensor("s_sqn", [128, 128], F32)
    qn2 = nc.alloc_sbuf_tensor("s_qn2", [128, 32], F32)
    nbv = nc.alloc_sbuf_tensor("s_negB", [128, 16], F32)
    zpv = nc.alloc_sbuf_tensor("s_zp", [128, 8], F32)
    rzv = nc.alloc_sbuf_tensor("s_rz", [128, 4], F32)
    prow = [nc.alloc_sbuf_tensor(f"s_prow{i}", [128, 2048], BF16) for i in range(2)]
    sgt = nc.alloc_sbuf_tensor("s_sgt", [128, 64], F32)
    sut = nc.alloc_sbuf_tensor("s_sut", [128, 64], F32)
    diagrz = nc.alloc_sbuf_tensor("s_diagrz", [128, 3 * 128], BF16)
    junk = nc.alloc_sbuf_tensor("s_junk", [128, 128], F32)
    epsv = nc.alloc_sbuf_tensor("s_epsv", [128, 4], F32)

    arena0 = (nc.sbuf_base + 31) & ~31
    top = nc.sbuf_top
    assert top - arena0 >= 194 * KB, f"arena too small: {top - arena0}"

    def at(name, off_kb, shape, dtype):
        return nc.alloc_sbuf_tensor_at(name, shape, dtype,
                                       offset=arena0 + int(off_kb * KB))

    ck = at("a_ck", 0, [128, 2048], BF16)
    sk = at("a_sk", 4, [128, 2048], BF16)
    KT = at("a_KT", 16, [128, 4 * 2048], BF16)
    VT = at("a_VT", 32, [128, NT * 512], BF16)
    wakv = at("a_wakv", 48, [128, MCN * 1024], BF16)
    xst = [at(f"a_xst{i}", 80 + 8 * i, [128, 2048], F32) for i in range(2)]
    xln = [at(f"a_xln{i}", 96 + 4 * i, [128, 2048], BF16) for i in range(2)]
    xlt = [at(f"a_xlt{i}", 104 + 4 * i, [128, 2048], BF16) for i in range(2)]
    krp = [at(f"a_krp{i}", 112 + 1 * i, [128, 512], BF16) for i in range(2)]
    ksq = xlt[0]
    cq = at("a_cq", 0, [128, 1024], BF16)
    sq = at("a_sq", 2, [128, 1024], BF16)
    masks = at("a_masks", 8, [128, 2048], F32)
    qT = at("a_qT", 48, [128, HEADS * OWN * 128], BF16)
    waq = at("a_waq", 114, [128, MCN * 2048], BF16)
    qrp = at("a_qrp", 178, [128, 2048], BF16)
    tt1 = at("a_tt1", 182, [128, 1024], F32)
    tt2 = at("a_tt2", 186, [128, 1024], F32)
    qrp2 = at("a_qrp2", 190, [128, 2048], BF16)
    qbf = at("a_qbf", 4, [128, 2048], BF16)
    krw = [at(f"a_krw{i}", 8 + i, [128, 512], BF16) for i in range(2)]
    PT = [at("a_PT0", 80, [128, NT * 1024], BF16),
          at("a_PT1", 114, [128, NT * 1024], BF16)]
    oT = at("a_oT", 146, [128, HEADS * 1024], BF16)
    prow2 = at("a_prow2", 0, [128, 2048], BF16)
    wao = at("a_wao", 0, [128, HEADS * 2048], BF16)
    x2lt = at("a_x2lt", 64, [128, MCN * 1024], BF16)
    xstb = [at(f"a_xstb{i}", 96 + 8 * i, [128, 2048], F32) for i in range(2)]
    rti = [at(f"a_rti{i}", 112 + 8 * i, [128, 2048], F32) for i in range(2)]
    x2ln = at("a_x2ln", 128, [128, 2048], BF16)
    x2ln1 = at("a_x2ln1", 132, [128, 2048], BF16)
    wgbf = [at(f"b_wgbf{i}", 0 + 4 * i, [128, 2048], BF16) for i in range(2)]
    wubf = [at(f"b_wubf{i}", 8 + 4 * i, [128, 2048], BF16) for i in range(2)]
    WgT = [at(f"b_WgT{i}", 16 + 4 * i, [128, 2048], BF16) for i in range(2)]
    WuT = [at(f"b_WuT{i}", 24 + 4 * i, [128, 2048], BF16) for i in range(2)]
    gsil = at("b_gsil", 32, [128, 1024], F32)
    guT = at("b_guT", 96, [128, HCN * 1024], BF16)
    wdbf = [at(f"d_wdbf{i}", 0 + 4 * i, [128, 2048], BF16) for i in range(2)]
    wdsl = [at(f"d_wdsl{i}", 8 + 4 * i, [128, 2048], BF16) for i in range(2)]
    rsl = [at(f"d_rsl{i}", 16 + 2 * i, [128, 512], F32) for i in range(8)]
    sdb = at("d_sdb", 32, [128, 2048], F32)
    outst = [at(f"d_outst{i}", 40 + 2 * i, [128, 512], F32) for i in range(2)]

    ps = nc.alloc_psum_tensor("ps", [128, 4096], F32)
    psb = ps.ap().bitcast(BF16)

    def bank(b, n=1):
        return ps.ap()[:, b * 512:(b + n) * 512]

    def bankb(b, n=1):
        return psb[:, b * 1024:(b + n) * 1024]

    R = lambda: Res()
    r_xst = [R(), R()]
    r_xln = [R(), R()]
    r_xlt = [R(), R()]
    r_krp = [R(), R()]
    r_stats = [R(), R()]
    r_lnv = [R(), R()]
    r_tabs = R()
    r_masks = R()
    r_const = R()
    r_bias = R()
    r_KT, r_VT, r_wakv, r_waq, r_qT, r_oT, r_wao = R(), R(), R(), R(), R(), R(), R()
    r_ksq = R()
    r_kgs, r_sqn = R(), R()
    r_qn2 = [R(), R()]
    r_nb = [R(), R()]
    r_zp = [R(), R()]
    r_rz = [R(), R()]
    r_prow = [R(), R(), R()]
    r_diag = [R(), R(), R()]
    r_p32 = [R(), R()]
    r_PT = [R(), R()]
    r_qrp, r_qrp2, r_tt1, r_tt2 = R(), R(), R(), R()
    r_qbf = R()
    r_krw = [R(), R()]
    r_x2lt, r_x2ln, r_x2ln1 = R(), R(), R()
    r_xstb = [R(), R()]
    r_rti = [R(), R()]
    r_rbuf = [R() for _ in range(OWN)]
    r_wdtbuf = R()
    r_wgbf = [R(), R()]
    r_wubf = [R(), R()]
    r_WgT = [R(), R()]
    r_WuT = [R(), R()]
    r_gsil, r_guT = R(), R()
    r_sdb, r_sscale = R(), R()
    r_wdbf = [R(), R()]
    r_wdsl = [R(), R()]
    r_rsl = [R() for _ in range(8)]
    r_outst = [R(), R()]
    r_junk = R()
    r_bank = [R() for _ in range(8)]

    def banks_r(b, n=1):
        return r_bank[b:b + n]

    def load(dst, src, res_w, res_r=(), q="Y"):
        dma = "DQ" if q == "Y" else "DG"
        return pr.op(q, lambda e: e.dma_start(dst, src),
                     reads=list(res_r), writes=[res_w], dma=dma)

    def store(dst, src, res_r, res_w=(), q="Q"):
        dma = "DG" if q == "Q" else "DQ"
        return pr.op(q, lambda e: e.dma_start(dst, src),
                     reads=list(res_r), writes=list(res_w), dma=dma)

    def ln_stats(xsrc, r_xsrc, rot):
        st = stats.ap()[:, rot * 32:rot * 32 + 32]
        rs = r_stats[rot]
        lv = lnv.ap()[:, rot * 8:rot * 8 + 8]
        rl = r_lnv[rot]
        sink = prow[rot]
        pr.op("S", lambda e, sink=sink: e.activation(
            sink.ap(), xsrc, AF.Square, accum_out=st[:, 0:1]),
            reads=[r_xsrc], writes=[rs, r_prow[rot]])
        pr.op("S", lambda e, sink=sink: e.activation(
            sink.ap(), xsrc, AF.Identity, accum_out=st[:, 2:3]),
            reads=[r_xsrc], writes=[rs, r_prow[rot]])
        pr.op("V", lambda e: e.tensor_scalar(st[:, 3:4], st[:, 2:3],
                                             1.0 / 2048.0, None, ALU.mult),
              reads=[rs], writes=[rs])
        pr.op("V", lambda e: e.scalar_tensor_tensor(
            st[:, 4:5], st[:, 2:3], 1.0 / 2048.0, st[:, 3:4], ALU.mult, ALU.mult),
            reads=[rs], writes=[rs])
        pr.op("V", lambda e: e.scalar_tensor_tensor(
            st[:, 5:6], st[:, 0:1], 1.0 / 2048.0, st[:, 4:5], ALU.mult, ALU.subtract),
            reads=[rs], writes=[rs])
        pr.op("S", lambda e: e.activation(lv[:, 0:1], st[:, 5:6], AF.Sqrt,
                                          bias=epsv.ap()[:, 0:1]),
              reads=[rs, r_const], writes=[rl])

    def ln_fin(xsrc, r_xsrc, xdst, r_xdst, rot):
        st = stats.ap()[:, rot * 32:rot * 32 + 32]
        rs = r_stats[rot]
        lv = lnv.ap()[:, rot * 8:rot * 8 + 8]
        rl = r_lnv[rot]
        pr.op("V", lambda e: e.reciprocal(lv[:, 1:2], lv[:, 0:1]), reads=[rl], writes=[rl])
        pr.op("V", lambda e: e.scalar_tensor_tensor(lv[:, 2:3], st[:, 3:4], -1.0,
                                                    lv[:, 1:2], ALU.mult, ALU.mult),
              reads=[rs, rl], writes=[rl])
        pr.op("S", lambda e: e.activation(xdst, xsrc, AF.Identity,
                                          bias=lv[:, 2:3], scale=lv[:, 1:2]),
              reads=[r_xsrc, rl], writes=[r_xdst])

    def ln_pass(xsrc, r_xsrc, xdst, r_xdst, rot):
        ln_stats(xsrc, r_xsrc, rot)
        ln_fin(xsrc, r_xsrc, xdst, r_xdst, rot)

    def transpose16(src_ap, r_src, dst_ten, dst_col_of, r_dst, pb0, n=16,
                    src_col_of=None, eng="V"):
        if src_col_of is None:
            src_col_of = lambda c: c * 128
        for c0 in range(0, n, 8):
            nb = min(8, n - c0)
            pbank = pb0 + (c0 // 8) % 2
            pv = bankb(pbank)
            for c in range(c0, c0 + nb):
                cc = c
                pr.op("T", lambda e, cc=cc, pv=pv: e.transpose(
                    pv[:, (cc % 8) * 128:(cc % 8) * 128 + 128],
                    src_ap[:, src_col_of(cc):src_col_of(cc) + 128], ident.ap()),
                    reads=[r_src, r_const], writes=[r_bank[pbank]])
            dcols = [dst_col_of(c) for c in range(c0, c0 + nb)]
            step = dcols[1] - dcols[0] if nb > 1 else 128
            assert all(dcols[i + 1] - dcols[i] == step for i in range(nb - 1))
            dap = bass.AP(dst_ten, dcols[0], [[dst_ten.shape[1], 128], [step, nb], [1, 128]])
            if eng == "V":
                pr.op("V", lambda e, pv=pv, dap=dap, nb=nb: e.tensor_copy(
                    dap, pv[:, 0:nb * 128]), reads=[r_bank[pbank]], writes=[r_dst])
            else:
                pr.op("S", lambda e, pv=pv, dap=dap, nb=nb: e.activation(
                    dap, pv[:, 0:nb * 128], AF.Identity),
                    reads=[r_bank[pbank]], writes=[r_dst])

    load(xst[0].ap(), dx[0:128, :], r_xst[0])
    load(xst[1].ap(), dx[128:256, :], r_xst[1])
    pr.op("V", lambda e: e.memset(epsv.ap(), EPS), writes=[r_const])
    pr.op("V", lambda e: e.memset(epsv.ap()[:, 2:4], 0.0), writes=[r_const])
    load(ident.ap(), dident, r_const)
    load(ones.ap(), dones, r_const)
    load(ck.ap(), dck, r_tabs)
    load(sk.ap(), dsk, r_tabs)

    for g in range(G):
        for part, src in ((0, dwak), (1, dwav)):
            sap = src[g].rearrange("(mc p) d -> p mc d", p=128)
            dap = bass.AP(wakv, part * 512 + g * 128, [[MCN * 1024, 128], [1024, MCN], [1, 128]])
            load(dap, sap, r_wakv, q="Q")

    kv_banks = [(0, 2), (2, 2)]

    r_xln3 = R()
    xln3 = [(xln[0], r_xln[0]), (xln[1], r_xln[1]), (qrp, r_xln3)]

    def a1_load(jt):
        rot = jt % 2
        load(xst[rot].ap(), dx[jt * 128:(jt + 1) * 128, :], r_xst[rot])

    def a1_stats(jt):
        rot = jt % 2
        ln_stats(xst[rot].ap(), r_xst[rot], rot)

    def a1_fin(jt):
        rot = jt % 2
        xd, r_xd = xln3[jt % 3]
        ln_fin(xst[rot].ap(), r_xst[rot], xd.ap(), r_xd, rot)

    def a1_kv(jt):
        rot = jt % 2
        xd, r_xd = xln3[jt % 3]
        transpose16(xd.ap(), r_xd, xlt[rot], lambda c: c * 128,
                    r_xlt[rot], 4)
        b0, nb = kv_banks[rot]
        kvp = bank(b0, nb)
        for mc in range(MCN):
            for piece in range(2):
                pr.op("T", lambda e, mc=mc, piece=piece, kvp=kvp, rot=rot: e.matmul(
                    kvp[:, piece * 512:(piece + 1) * 512],
                    xlt[rot].ap()[:, mc * 128:(mc + 1) * 128],
                    wakv.ap()[:, mc * 1024 + piece * 512: mc * 1024 + (piece + 1) * 512],
                    start=(mc == 0), stop=(mc == MCN - 1)),
                    reads=[r_xlt[rot], r_wakv], writes=banks_r(b0 + piece))
        pr.op("V", lambda e, kvp=kvp, rot=rot: e.tensor_copy(
            krw[rot].ap(), kvp[:, 0:512]),
            reads=banks_r(b0, 1), writes=[r_krw[rot]])
        kp = krw[rot].ap()
        kswap = kp.rearrange("p (g two half) -> p g two half", two=2, half=64)[:, :, ::-1, :]
        ckt = ck.ap()[:, jt * 128:(jt + 1) * 128].rearrange(
            "p (two half) -> p two half", two=2).unsqueeze(1).broadcast_to([128, G, 2, 64])
        skt = sk.ap()[:, jt * 128:(jt + 1) * 128].rearrange(
            "p (two half) -> p two half", two=2).unsqueeze(1).broadcast_to([128, G, 2, 64])
        t1v = tt1.ap().bitcast(BF16)[:, 0:512]
        t2v = tt2.ap().bitcast(BF16)[:, 0:512]
        kro = krp[rot].ap()
        pr.op("V", lambda e, kp=kp, ckt=ckt, t1v=t1v, rot=rot: e.tensor_tensor(
            t1v.rearrange("p (g two half) -> p g two half", two=2, half=64),
            kp.rearrange("p (g two half) -> p g two half", two=2, half=64), ckt, ALU.mult),
            reads=[r_krw[rot], r_tabs], writes=[r_tt1])
        pr.op("V", lambda e, kswap=kswap, skt=skt, t2v=t2v, rot=rot: e.tensor_tensor(
            t2v.rearrange("p (g two half) -> p g two half", two=2, half=64),
            kswap, skt, ALU.mult),
            reads=[r_krw[rot], r_tabs], writes=[r_tt2])
        pr.op("V", lambda e, t1v=t1v, t2v=t2v, kro=kro: e.tensor_tensor(
            kro, t1v, t2v, ALU.add), reads=[r_tt1, r_tt2], writes=[r_krp[rot]])
        pr.op("V", lambda e, kvp=kvp, jt=jt: e.tensor_copy(
            VT.ap()[:, jt * 512:(jt + 1) * 512], kvp[:, 512:1024]),
            reads=banks_r(b0 + 1, 1), writes=[r_VT])

    def a1_ktr(jt):
        rot = jt % 2
        transpose16(krp[rot].ap(), r_krp[rot], KT,
                    lambda c, jt=jt: c * 2048 + jt * 128, r_KT, 6, n=4)

    a1_stats(0)
    a1_fin(0)
    a1_stats(1)
    a1_fin(1)
    for jt in range(NT):
        if jt + 2 < NT:
            a1_load(jt + 2)
        a1_kv(jt)
        if jt >= 1:
            a1_ktr(jt - 1)
        if jt + 2 < NT:
            a1_stats(jt + 2)
            a1_fin(jt + 2)
    a1_ktr(NT - 1)
    for g in range(G):
        pr.op("S", lambda e, g=g: e.activation(ksq.ap(), KT.ap()[:, g * 2048:(g + 1) * 2048],
                                               AF.Square),
              reads=[r_KT], writes=[r_ksq, r_xlt[0]])
        for piece in range(4):
            pr.op("T", lambda e, piece=piece: e.matmul(
                bank(piece), ones.ap(), ksq.ap()[:, piece * 512:(piece + 1) * 512],
                start=True, stop=True),
                reads=[r_ksq, r_const], writes=banks_r(piece))
        pr.op("V", lambda e, g=g: e.tensor_reduce(
            kgs.ap()[:, g:g + 1], bank(0, 4), AX.X, ALU.max),
            reads=banks_r(0, 4), writes=[r_kgs])
    pr.op("S", lambda e: e.activation(kgs.ap()[:, 4:8], kgs.ap()[:, 0:4], AF.Sqrt),
          reads=[r_kgs], writes=[r_kgs])

    doutb = dout.bitcast(BF16)
    band = [0]

    def dump_bf(src_ap, r_src):
        n = src_ap.shape[1]
        for c0 in range(0, n, 4096):
            w = min(4096, n - c0)
            b = band[0]
            band[0] += 1
            store(doutb[b * 128:(b + 1) * 128, 0:w], src_ap[:, c0:c0 + w], [r_src])

    def dump_f32(src_ap, r_src):
        n = src_ap.shape[1]
        for c0 in range(0, n, 2048):
            w = min(2048, n - c0)
            b = band[0]
            band[0] += 1
            store(dout[b * 128:(b + 1) * 128, 0:w], src_ap[:, c0:c0 + w], [r_src])

    if upto == "a1":
        dump_bf(KT.ap(), r_KT)
        dump_bf(VT.ap(), r_VT)
        dump_f32(kgs.ap(), r_kgs)
        dump_bf(xln[1].ap(), r_xln[1])
        dump_f32(stats.ap(), r_stats[1])
        dump_f32(lnv.ap(), r_lnv[1])
        return nc, pr

    for g in range(G):
        for h in range(H):
            sap = dwaq[g, h].rearrange("(mc p) d -> p mc d", p=128)
            dap = bass.AP(waq, (g * H + h) * 128, [[MCN * 2048, 128], [2048, MCN], [1, 128]])
            load(dap, sap, r_waq, q="Q")

    qrpb = [(qrp, r_qrp), (qrp2, r_qrp2)]

    def a2q_load(k):
        rot = k % 2
        load(xst[rot].ap(), dxo[k * 128:(k + 1) * 128, :], r_xst[rot])

    def a2q_stats(k):
        rot = k % 2
        ln_stats(xst[rot].ap(), r_xst[rot], rot)

    def a2q_fin(k):
        rot = k % 2
        ln_fin(xst[rot].ap(), r_xst[rot], xln[rot].ap(), r_xln[rot], rot)

    def a2q_xlt(k):
        rot = k % 2
        transpose16(xln[rot].ap(), r_xln[rot], xlt[rot], lambda c: c * 128,
                    r_xlt[rot], 4)

    def a2q_mm(k):
        rot = k % 2
        for piece in range(4):
            for mc in range(MCN):
                pr.op("T", lambda e, mc=mc, piece=piece, rot=rot: e.matmul(
                    bank(piece),
                    xlt[rot].ap()[:, mc * 128:(mc + 1) * 128],
                    waq.ap()[:, mc * 2048 + piece * 512: mc * 2048 + (piece + 1) * 512],
                    start=(mc == 0), stop=(mc == MCN - 1)),
                    reads=[r_xlt[rot], r_waq], writes=banks_r(piece))

    def a2q_rope(k):
        rot = k % 2
        qo, r_qo = qrpb[rot]
        cqt = cq.ap()[:, k * 128:(k + 1) * 128].rearrange(
            "p (two half) -> p two half", two=2).unsqueeze(1).broadcast_to([128, HEADS, 2, 64])
        sqt = sq.ap()[:, k * 128:(k + 1) * 128].rearrange(
            "p (two half) -> p two half", two=2).unsqueeze(1).broadcast_to([128, HEADS, 2, 64])
        pr.op("V", lambda e: e.tensor_copy(qbf.ap(), bank(0, 4)),
              reads=banks_r(0, 4), writes=[r_qbf, r_tabs])
        qp = qbf.ap()
        qswap = qp.rearrange("p (gh two half) -> p gh two half",
                             two=2, half=64)[:, :, ::-1, :]
        t1v = tt1.ap().bitcast(BF16)
        t2v = tt2.ap().bitcast(BF16)
        pr.op("V", lambda e, qp=qp, cqt=cqt, t1v=t1v: e.tensor_tensor(
            t1v.rearrange("p (gh two half) -> p gh two half", two=2, half=64),
            qp.rearrange("p (gh two half) -> p gh two half", two=2, half=64),
            cqt, ALU.mult),
            reads=[r_qbf, r_tabs], writes=[r_tt1])
        pr.op("V", lambda e, qswap=qswap, sqt=sqt, t2v=t2v: e.tensor_tensor(
            t2v.rearrange("p (gh two half) -> p gh two half", two=2, half=64),
            qswap, sqt, ALU.mult),
            reads=[r_qbf, r_tabs], writes=[r_tt2])
        pr.op("V", lambda e, qo=qo, t1v=t1v, t2v=t2v: e.tensor_tensor(
            qo.ap(), t1v, t2v, ALU.add),
            reads=[r_tt1, r_tt2], writes=[r_qo])
        pr.op("S", lambda e, qo=qo, t1v=t1v: e.activation(
            t1v, qo.ap(), AF.Square), reads=[r_qo], writes=[r_tt1])
        pr.op("V", lambda e, rot=rot, t1v=t1v: e.tensor_reduce(
            qn2.ap()[:, rot * 16:(rot + 1) * 16],
            t1v.rearrange("p (h d) -> p h d", d=128), AX.X, ALU.add),
            reads=[r_tt1], writes=[r_qn2[rot]])
        pr.op("S", lambda e, k=k, rot=rot: e.activation(
            sqn.ap()[:, k * 16:(k + 1) * 16], qn2.ap()[:, rot * 16:(rot + 1) * 16],
            AF.Sqrt), reads=[r_qn2[rot]], writes=[r_sqn])

    def a2q_qtr(k):
        rot = k % 2
        qo, r_qo = qrpb[rot]
        transpose16(qo.ap(), r_qo, qT, lambda c, k=k: c * 1024 + k * 128,
                    r_qT, 6, n=HEADS)

    a2q_load(0)
    load(cq.ap(), dcq, r_tabs)
    load(sq.ap(), dsq, r_tabs)
    pr.op("Y", lambda e: e.dma_start(masks.ap(), dmask),
          writes=[r_masks, r_tabs, r_krw[0], r_krw[1]], dma="DQ")
    a2q_load(1)
    a2q_stats(0)
    a2q_fin(0)
    a2q_stats(1)
    a2q_fin(1)
    a2q_xlt(0)
    for k in range(OWN):
        if k + 2 < OWN:
            a2q_load(k + 2)
        a2q_mm(k)
        a2q_rope(k)
        if k + 1 < OWN:
            a2q_xlt(k + 1)
        if k >= 1:
            a2q_qtr(k - 1)
        if k + 2 < OWN:
            a2q_stats(k + 2)
            a2q_fin(k + 2)
    a2q_qtr(OWN - 1)

    if upto == "a2q":
        dump_bf(qT.ap(), r_qT)
        dump_f32(sqn.ap(), r_sqn)
        return nc, pr

    slot_seq = [0]

    def next_slot():
        s = slot_seq[0]
        slot_seq[0] = (s + 1) % 3
        return s

    prow3 = [prow[0], prow[1], prow2]

    def att_negb(h):
        g = h // 4
        hr = h % 2
        nb = nbv.ap()[:, hr * 8:hr * 8 + 8]
        for k in range(OWN):
            pr.op("V", lambda e, k=k, h=h, g=g, nb=nb: e.scalar_tensor_tensor(
                nb[:, k:k + 1], sqn.ap()[:, k * 16 + h:k * 16 + h + 1], -1.0,
                kgs.ap()[:, 4 + g:5 + g], ALU.mult, ALU.mult),
                reads=[r_sqn, r_kgs], writes=[r_nb[hr]])

    def att_scores(h, k, sl, widths):
        g = h // 4
        for si, s in enumerate(sl):
            wdt = widths[si]
            for c0 in range(0, wdt, 512):
                cw = min(512, wdt - c0)
                jcol = si * 1024 + c0
                pr.op("T", lambda e, s=s, c0=c0, cw=cw, jcol=jcol, h=h, k=k, g=g: e.matmul(
                    bank(2 * s, 2)[:, c0:c0 + cw],
                    qT.ap()[:, h * 1024 + k * 128: h * 1024 + (k + 1) * 128],
                    KT.ap()[:, g * 2048 + jcol: g * 2048 + jcol + cw],
                    start=True, stop=True),
                    reads=[r_qT, r_KT], writes=banks_r(2 * s + c0 // 512))

    def att_chain(h, k, sl, widths):
        hr = h % 2
        par = h % 2
        nb = nbv.ap()[:, hr * 8:hr * 8 + 8]
        J = 2 * k + 2
        W = J * 128
        zrot = k % 2
        zp = zpv.ap()[:, zrot * 4: zrot * 4 + 4]
        r_z = r_zp[zrot]
        rz = rzv.ap()[:, zrot * 2:zrot * 2 + 2]
        r_r = r_rz[zrot]
        mi = (W - 256) // 1024
        moff = (W - 256) % 1024
        pr.op("V", lambda e, s=sl[mi], moff=moff, k=k: e.tensor_tensor(
            bank(2 * s, 2)[:, moff:moff + 256],
            bank(2 * s, 2)[:, moff:moff + 256],
            masks.ap()[:, k * 256:(k + 1) * 256], ALU.add),
            reads=[r_masks] + banks_r(2 * sl[mi] + moff // 512),
            writes=banks_r(2 * sl[mi] + moff // 512))
        pw = prow3[k % 3]
        r_pw = r_prow[k % 3]
        for si, s in enumerate(sl):
            wdt = widths[si]
            pr.op("S", lambda e, s=s, si=si, wdt=wdt, zp=zp, nb=nb, k=k, pw=pw: e.activation(
                pw.ap()[:, si * 1024: si * 1024 + wdt], bank(2 * s, 2)[:, 0:wdt],
                AF.Exp, bias=nb[:, k:k + 1], accum_out=zp[:, si:si + 1]),
                reads=banks_r(2 * s, (wdt + 511) // 512) + [r_nb[hr]],
                writes=[r_pw, r_z])
        if len(sl) == 2:
            pr.op("V", lambda e, zp=zp: e.tensor_tensor(
                zp[:, 2:3], zp[:, 0:1], zp[:, 1:2], ALU.add),
                reads=[r_z], writes=[r_z])
            zi = 2
        else:
            zi = 0
        pr.op("V", lambda e, zp=zp, rz=rz, zi=zi: e.reciprocal(rz[:, 0:1], zp[:, zi:zi + 1]),
              reads=[r_z], writes=[r_r])
        pr.op("V", lambda e, k=k, rz=rz: e.tensor_scalar(
            diagrz.ap()[:, (k % 3) * 128:(k % 3) * 128 + 128], ident.ap(),
            rz[:, 0:1], None, ALU.mult),
            reads=[r_r, r_const], writes=[r_diag[k % 3]])

    def att_pt(h, k, sl):
        par = h % 2
        pt_t = PT[par]
        r_pt = r_PT[par]
        pw = prow3[k % 3]
        dg = diagrz.ap()[:, (k % 3) * 128:(k % 3) * 128 + 128]
        J = 2 * k + 2
        for j in range(J):
            slot = sl[0] if j < 8 else sl[1]
            pb = 2 * slot + (j % 8) // 4
            pv = bank(pb)
            pr.op("T", lambda e, j=j, pv=pv, pw=pw, dg=dg: e.matmul(
                pv[:, (j % 4) * 128:(j % 4) * 128 + 128],
                pw.ap()[:, j * 128:(j + 1) * 128], dg, start=True, stop=True),
                reads=[r_prow[k % 3], r_diag[k % 3]], writes=[r_bank[pb]])
        for j0 in range(0, J, 8):
            nj = min(8, J - j0)
            slot = sl[0] if j0 < 8 else sl[1]
            dap = bass.AP(pt_t, j0 * 1024 + k * 128,
                          [[NT * 1024, 128], [1024, nj], [1, 128]])
            eng = "V" if (j0 // 8 + k) % 2 == 0 else "S"
            if eng == "V":
                pr.op("V", lambda e, slot=slot, nj=nj, dap=dap: e.tensor_copy(
                    dap, bank(2 * slot, 2)[:, 0:nj * 128]),
                    reads=banks_r(2 * slot, 2), writes=[r_pt])
            else:
                pr.op("S", lambda e, slot=slot, nj=nj, dap=dap: e.activation(
                    dap, bank(2 * slot, 2)[:, 0:nj * 128], AF.Identity),
                    reads=banks_r(2 * slot, 2), writes=[r_pt])

    def att_pv_ops(h):
        g = h // 4
        par = h % 2
        pt_t = PT[par]
        r_pt = r_PT[par]
        thunks = []
        for jt in range(NT):
            k0 = jt // 2
            c0 = k0 * 128
            pieces = []
            if c0 < 512:
                pieces.append((c0, 512))
            pieces.append((max(c0, 512), 1024))
            for (a, b) in pieces:
                pbank = 6 + (a // 512)

                def th(jt=jt, a=a, b=b, pbank=pbank):
                    pr.op("T", lambda e, jt=jt, a=a, b=b: e.matmul(
                        bank(6, 2)[:, a:b],
                        VT.ap()[:, jt * 512 + g * 128: jt * 512 + (g + 1) * 128],
                        pt_t.ap()[:, jt * 1024 + a: jt * 1024 + b],
                        start=(jt == 0), stop=(jt == (7 if pbank == 6 else 15))),
                        reads=[r_VT, r_pt], writes=[r_bank[pbank]])
                thunks.append(th)
        return thunks

    def att_oevac(h):
        pr.op("S", lambda e, h=h: e.activation(
            oT.ap()[:, h * 1024:(h + 1) * 1024], bank(6, 2), AF.Identity),
            reads=banks_r(6, 2), writes=[r_oT])

    pending_pv = []
    pending_h = None
    for h in range(HEADS):
        att_negb(h)
        slk = {}
        wdk = {}
        for k in range(OWN):
            W = (2 * k + 2) * 128
            nslots = 1 if W <= 1024 else 2
            sl = [next_slot() for _ in range(nslots)]
            slk[k] = sl
            wdk[k] = [min(W, 1024)] + ([W - 1024] if W > 1024 else [])
            att_scores(h, k, sl, wdk[k])
            for _ in range(3):
                if pending_pv:
                    pending_pv.pop(0)()
            att_chain(h, k, sl, wdk[k])
            if k >= 2:
                att_pt(h, k - 2, slk[k - 2])
        att_pt(h, OWN - 2, slk[OWN - 2])
        att_pt(h, OWN - 1, slk[OWN - 1])
        while pending_pv:
            pending_pv.pop(0)()
        if pending_h is not None:
            att_oevac(pending_h)
        pending_pv = att_pv_ops(h)
        pending_h = h
    while pending_pv:
        pending_pv.pop(0)()
    att_oevac(pending_h)

    if upto == "a2att":
        dump_bf(oT.ap(), r_oT)
        return nc, pr

    pr.barrier()
    st_wao = at("a_stwao", 136, [128, 2048], F32)
    r_stw = R()
    r_wao_hh = [R() for _ in range(HEADS)]
    for h in range(HEADS):
        g, hh = h // 4, h % 4
        dst = bass.AP(wao, h * 2048, [[HEADS * 2048, 128], [1, 2048]])
        if h % 2 == 1:
            load(dst, dwao[g, hh], r_wao_hh[h], q="Q")
        else:
            load(st_wao.ap(), dwao[g, hh], r_stw, q="Y")
            pr.op("S", lambda e, dst=dst: e.copy(dst, st_wao.ap()),
                  reads=[r_stw], writes=[r_wao_hh[h], r_stw])
    def a2b_mm(k):
        seto = (k % 2) * 4
        for h in range(HEADS):
            for piece in range(4):
                pr.op("T", lambda e, h=h, piece=piece, k=k, seto=seto: e.matmul(
                    bank(seto + piece),
                    oT.ap()[:, h * 1024 + k * 128: h * 1024 + (k + 1) * 128],
                    wao.ap()[:, h * 2048 + piece * 512: h * 2048 + (piece + 1) * 512],
                    start=(h == 0), stop=(h == HEADS - 1)),
                    reads=[r_oT, r_wao_hh[h]], writes=banks_r(seto + piece))

    x2lnbuf = [(x2ln, r_x2ln), (x2ln1, r_x2ln1)]
    load(xstb[0].ap(), dxo[0:128, :], r_xstb[0])
    for k in range(OWN):
        rot = k % 2
        seto = rot * 4
        if k + 1 < OWN:
            load(xstb[1 - rot].ap(), dxo[(k + 1) * 128:(k + 2) * 128, :],
                 r_xstb[1 - rot], res_r=[r_xstb[1 - rot]])
        a2b_mm(k)
        if k >= 1:
            ln_fin(rti[1 - rot].ap(), r_rti[1 - rot],
                   x2lnbuf[(k - 1) % 2][0].ap(), x2lnbuf[(k - 1) % 2][1],
                   (k - 1) % 2)
            transpose16(x2lnbuf[(k - 1) % 2][0].ap(), x2lnbuf[(k - 1) % 2][1],
                        x2lt, lambda c, kk=k - 1: c * 1024 + kk * 128,
                        r_x2lt, ((k - 1) % 2) * 4)
        pr.op("V", lambda e, rot=rot, seto=seto: e.tensor_tensor(
            rti[rot].ap(), bank(seto, 4), xstb[rot].ap(), ALU.add),
            reads=banks_r(seto, 4) + [r_xstb[rot]], writes=[r_rti[rot]])
        ln_stats(rti[rot].ap(), r_rti[rot], rot)
        store(drbuf[k * 128:(k + 1) * 128, :], rti[rot].ap(), [r_rti[rot]],
              res_w=[r_rbuf[k]])
    ln_fin(rti[(OWN - 1) % 2].ap(), r_rti[(OWN - 1) % 2],
           x2lnbuf[(OWN - 1) % 2][0].ap(), x2lnbuf[(OWN - 1) % 2][1],
           (OWN - 1) % 2)
    transpose16(x2lnbuf[(OWN - 1) % 2][0].ap(), x2lnbuf[(OWN - 1) % 2][1],
                x2lt, lambda c, kk=OWN - 1: c * 1024 + kk * 128,
                r_x2lt, ((OWN - 1) % 2) * 4)

    if upto == "a2b":
        dump_bf(x2lt.ap(), r_x2lt)
        return nc, pr

    pr.barrier()
    load(sgt.ap()[:, 0:HCN], dsg, r_sscale)
    load(sut.ap()[:, 0:HCN], dsu, r_sscale)

    def b1_loads(hc):
        rot = hc % 2
        for (dsrc, dst, r_dst) in ((dwqg, wgbf, r_wgbf), (dwqu, wubf, r_wubf)):
            src = dsrc[:, hc * 128:(hc + 1) * 128].rearrange(
                "(mc p) h -> p mc h", p=128)
            load(dst[rot].ap().rearrange("p (mc h) -> p mc h", h=128), src,
                 r_dst[rot], q="Q")

    def b1_deq(hc):
        rot = hc % 2
        for (stg, r_stg, WT, rWT) in ((wgbf, r_wgbf, WgT, r_WgT),
                                      (wubf, r_wubf, WuT, r_WuT)):
            pr.op("V", lambda e, rot=rot, stg=stg, WT=WT: e.tensor_scalar(
                WT[rot].ap(), stg[rot].ap(), -128.0, None, ALU.add),
                reads=[r_stg[rot]], writes=[rWT[rot]])

    def b1_mm(hc):
        rot = hc % 2
        seta = 0 if hc % 2 == 0 else 4
        for WT, rWT, boff in ((WgT, r_WgT, 0), (WuT, r_WuT, 2)):
            for mc in range(MCN):
                for piece in range(2):
                    pr.op("T", lambda e, mc=mc, piece=piece, rot=rot, seta=seta,
                          WT=WT, boff=boff: e.matmul(
                        bank(seta + boff + piece),
                        WT[rot].ap()[:, mc * 128:(mc + 1) * 128],
                        x2lt.ap()[:, mc * 1024 + piece * 512: mc * 1024 + (piece + 1) * 512],
                        start=(mc == 0), stop=(mc == MCN - 1)),
                        reads=[rWT[rot], r_x2lt], writes=banks_r(seta + boff + piece))

    def b1_post(hc):
        seta = 0 if hc % 2 == 0 else 4
        pr.op("S", lambda e, seta=seta, hc=hc: e.activation(
            gsil.ap(), bank(seta, 2), AF.Silu, scale=sgt.ap()[:, hc:hc + 1]),
            reads=banks_r(seta, 2) + [r_sscale], writes=[r_gsil])
        pr.op("V", lambda e, seta=seta, hc=hc: e.scalar_tensor_tensor(
            guT.ap()[:, hc * 1024:(hc + 1) * 1024], bank(seta + 2, 2),
            sut.ap()[:, hc:hc + 1], gsil.ap(), ALU.mult, ALU.mult),
            reads=banks_r(seta + 2, 2) + [r_gsil, r_sscale], writes=[r_guT])

    b1_loads(0)
    b1_deq(0)
    for hc in range(HCN):
        if hc + 1 < HCN:
            b1_loads(hc + 1)
        b1_mm(hc)
        b1_post(hc)
        if hc + 1 < HCN:
            b1_deq(hc + 1)

    if upto == "b1":
        dump_bf(guT.ap()[:, 0:32768], r_guT)
        return nc, pr

    pr.barrier()
    NGRP = 11
    sdsrc = bass.AP(dsd.tensor, 0, [[0, 128], [1, M]])
    load(sdb.ap(), sdsrc, r_sdb)

    def down_wload(mc4, hcg):
        rot = (mc4 * NGRP + hcg) % 2
        src = dwqd[4 * hcg * 128:(4 * hcg + 4) * 128,
                   mc4 * 512:(mc4 + 1) * 512].rearrange(
            "(hh p) m -> p hh m", p=128)
        load(wdbf[rot].ap().rearrange("p (hh m) -> p hh m", m=512), src,
             r_wdbf[rot], q="Q")

    def down_deq(mc4, hcg):
        rot = (mc4 * NGRP + hcg) % 2
        pr.op("V", lambda e, rot=rot: e.tensor_scalar(
            wdsl[rot].ap(), wdbf[rot].ap(), -128.0, None, ALU.add),
            reads=[r_wdbf[rot]], writes=[r_wdsl[rot]])

    def down_mm(mc4, hcg):
        rot = (mc4 * NGRP + hcg) % 2
        for hh in range(4):
            hc = 4 * hcg + hh
            for t in range(OWN):
                pr.op("T", lambda e, hc=hc, hh=hh, t=t, rot=rot: e.matmul(
                    bank(t),
                    guT.ap()[:, hc * 1024 + t * 128: hc * 1024 + (t + 1) * 128],
                    wdsl[rot].ap()[:, hh * 512:(hh + 1) * 512],
                    start=(hc == 0), stop=(hc == HCN - 1)),
                    reads=[r_guT, r_wdsl[rot]], writes=[r_bank[t]])

    for t in range(OWN):
        load(rsl[t].ap(), drbuf[t * 128:(t + 1) * 128, 0:512], r_rsl[t])
    down_wload(0, 0)
    down_deq(0, 0)
    for mc4 in range(4):
        for hcg in range(NGRP):
            nxt = (mc4, hcg + 1) if hcg + 1 < NGRP else (
                (mc4 + 1, 0) if mc4 + 1 < 4 else None)
            if nxt:
                down_wload(*nxt)
            down_mm(mc4, hcg)
            if nxt:
                down_deq(*nxt)
        for t in range(OWN):
            orot = t % 2
            pr.op("V", lambda e, t=t, orot=orot, mc4=mc4: e.tensor_tensor(
                outst[orot].ap(), bank(t),
                sdb.ap()[:, mc4 * 512:(mc4 + 1) * 512], ALU.mult),
                reads=[r_bank[t], r_sdb], writes=[r_outst[orot]])
            pr.op("V", lambda e, t=t, orot=orot: e.tensor_tensor(
                outst[orot].ap(), outst[orot].ap(), rsl[t].ap(), ALU.add),
                reads=[r_rsl[t], r_outst[orot]], writes=[r_outst[orot]])
            store(dout[t * 128:(t + 1) * 128, mc4 * 512:(mc4 + 1) * 512],
                  outst[orot].ap(), [r_outst[orot]])
            if mc4 + 1 < 4:
                load(rsl[t].ap(),
                     drbuf[t * 128:(t + 1) * 128, (mc4 + 1) * 512:(mc4 + 2) * 512],
                     r_rsl[t], res_r=[r_rsl[t]])

    return nc, pr




def build(upto="all"):
    nc, pr = build_nc(upto)
    sems = {}
    from contextlib import ExitStack
    ctx = ExitStack()
    for d in Prog.DOMS:
        sems[d] = ctx.enter_context(nc.semaphore(f"sem_{d}"))
    block = ctx.enter_context(nc.Block())
    try:
        pr.finish(nc, block, sems)
    finally:
        ctx.close()
    return nc


_cache = {}


def _host_tables():
    if "tabs" in _cache:
        return _cache["tabs"]
    half = D // 2
    pos = np.arange(T, dtype=np.float64)[:, None]
    freqs = (10000.0 ** (-np.arange(half, dtype=np.float64) / half))[None, :]
    ang = pos * freqs
    cosa = (np.cos(ang) * S4).astype(np.float32)
    sina = (np.sin(ang) * S4).astype(np.float32)

    def mk_tabs(tiles):
        n = len(tiles)
        C = np.empty((128, n * 128), np.float32)
        Sg = np.empty((128, n * 128), np.float32)
        for idx, t in enumerate(tiles):
            rows = slice(t * 128, (t + 1) * 128)
            C[:, idx * 128: idx * 128 + 64] = cosa[rows]
            C[:, idx * 128 + 64: idx * 128 + 128] = cosa[rows]
            Sg[:, idx * 128: idx * 128 + 64] = -sina[rows]
            Sg[:, idx * 128 + 64: idx * 128 + 128] = sina[rows]
        return C, Sg

    ck_t, sk_t = mk_tabs(list(range(NT)))
    cq0, sq0 = mk_tabs([2 * k for k in range(OWN)])
    cq1, sq1 = mk_tabs([2 * k + 1 for k in range(OWN)])

    ii = np.arange(128)[:, None]
    jj = np.arange(128)[None, :]
    tri = np.where(ii < jj, np.float32(NEG), np.float32(0.0))
    full = np.full((128, 128), np.float32(NEG))
    zero = np.zeros((128, 128), np.float32)
    m0 = np.empty((128, OWN * 256), np.float32)
    m1 = np.empty((128, OWN * 256), np.float32)
    for k in range(OWN):
        m0[:, k * 256:k * 256 + 128] = tri
        m0[:, k * 256 + 128:k * 256 + 256] = full
        m1[:, k * 256:k * 256 + 128] = zero
        m1[:, k * 256 + 128:k * 256 + 256] = tri
    cast = lambda a: a.astype(ml_dtypes.bfloat16)
    _cache["tabs"] = (cast(ck_t), cast(sk_t), (cast(cq0), cast(sq0)),
                      (cast(cq1), cast(sq1)), (m0, m1))
    return _cache["tabs"]


def kernel(L0_attn_x, L0_attn_w_aq, L0_attn_w_ak, L0_attn_w_av, L0_attn_w_ao,
           L0_ffn_wq_gate, L0_ffn_s_gate, L0_ffn_wq_up, L0_ffn_s_up,
           L0_ffn_wq_down, L0_ffn_s_down):
    x = np.ascontiguousarray(np.asarray(L0_attn_x, dtype=np.float32))
    waq = np.ascontiguousarray(np.asarray(L0_attn_w_aq, dtype=np.float32))
    wak = np.ascontiguousarray(np.asarray(L0_attn_w_ak, dtype=np.float32))
    wav = np.ascontiguousarray(np.asarray(L0_attn_w_av, dtype=np.float32))
    wao = np.ascontiguousarray(np.asarray(L0_attn_w_ao, dtype=np.float32))
    wqg = np.ascontiguousarray(np.asarray(L0_ffn_wq_gate, dtype=np.int32))
    wqu = np.ascontiguousarray(np.asarray(L0_ffn_wq_up, dtype=np.int32))
    wqd = np.ascontiguousarray(np.asarray(L0_ffn_wq_down, dtype=np.int32))
    sg = np.asarray(L0_ffn_s_gate, dtype=np.float32)
    su = np.asarray(L0_ffn_s_up, dtype=np.float32)
    sd = np.asarray(L0_ffn_s_down, dtype=np.float32)

    wqg_t = np.ascontiguousarray(wqg.T)
    wqu_t = np.ascontiguousarray(wqu.T)
    wqd_t = np.ascontiguousarray(wqd.T)
    sg_t = np.ascontiguousarray(sg.reshape(HCN, 128).T)
    su_t = np.ascontiguousarray(su.reshape(HCN, 128).T)
    sd_r = np.ascontiguousarray(sd.reshape(1, M))
    ck_t, sk_t, cqsq0, cqsq1, (m0, m1) = _host_tables()
    ident = np.eye(128, dtype=ml_dtypes.bfloat16)
    ones = np.ones((128, 128), dtype=ml_dtypes.bfloat16)

    if "nc" not in _cache:
        _cache["nc"] = build("all")
    nc = _cache["nc"]

    in_maps = []
    row_maps = []
    for c in range(8):
        b, p = c // 2, c % 2
        tiles = [2 * k + p for k in range(OWN)]
        rows = np.concatenate([np.arange(t * 128, (t + 1) * 128) for t in tiles])
        row_maps.append((b, rows))
        cqt, sqt = cqsq0 if p == 0 else cqsq1
        in_maps.append({
            "x_kv": x[b],
            "x_own": np.ascontiguousarray(x[b][rows]),
            "w_aq": waq, "w_ak": wak, "w_av": wav, "w_ao": wao,
            "wq_gate_t": wqg_t, "wq_up_t": wqu_t, "wq_down_t": wqd_t,
            "s_gate_t": sg_t, "s_up_t": su_t, "s_down_r": sd_r,
            "tab_ck": ck_t, "tab_sk": sk_t, "tab_cq": cqt, "tab_sq": sqt,
            "masks": m0 if p == 0 else m1,
            "ident": ident, "ones_bf": ones,
        })

    res = run_bass_kernel_spmd(nc, in_maps, core_ids=list(range(8)))
    out = np.empty((B, T, M), np.float32)
    for c in range(8):
        b, rows = row_maps[c]
        out[b][rows] = res.results[c]["out"]
    return out
